# version 20
# baseline (speedup 1.0000x reference)
import numpy as np
import ml_dtypes
import concourse.bass as bass
import concourse.mybir as mybir
from concourse.bass_utils import run_bass_kernel_spmd
from contextlib import ExitStack

F32 = mybir.dt.float32
BF16 = mybir.dt.bfloat16
AF = mybir.ActivationFunctionType
OP = mybir.AluOpType
AX = mybir.AxisListType

L, D, H, DK = 4096, 1024, 4, 256
TA = 256
NE, LAT, INTER, SHARED = 8, 512, 512, 4096

ENGS = ("pe", "act", "dve", "sp", "pool")
BLOCKMAP = {"pe": "tensor", "act": "scalar", "dve": "vector", "sp": "sync", "pool": "gpsimd"}


class Buf:
    __slots__ = ("w", "r", "name")
    def __init__(self, name=""):
        self.w, self.r, self.name = None, [], name


class Prog:
    HAZ = ("dve", "act", "pool")

    def __init__(self):
        self.q = {e: [] for e in ENGS}
        self.cnt = {e: 0 for e in ENGS}
        self.dcnt = {"ld": 0, "st": 0}
        self.hw = {}
        self.sems = {}
        self.pend = {e: set() for e in ENGS}
        self.src = {}
        self.nnop = 0
        self.defer = None
        self.bgq = []

    def _wait(self, eng, key, val):
        if key.startswith(("dl:", "ds:")):
            val = max(val, self.dtot[key][1])
            if val <= self.hw.get((eng, key), 0):
                return
            self.hw[(eng, key)] = val
            self.q[eng].append(("w", key, val))
            return
        if eng == "pe" and key == "pe":
            return
        if val <= self.hw.get((eng, key), 0):
            return
        self.hw[(eng, key)] = val
        self.q[eng].append(("w", key, val))

    def _deps(self, eng, reads, writes):
        for b in reads:
            if b.w:
                self._wait(eng, b.w[0], b.w[1])
        for b in writes:
            if b.w:
                self._wait(eng, b.w[0], b.w[1])
            for (re, rc) in b.r:
                self._wait(eng, re, rc)

    def emit(self, eng, fn, reads=(), writes=()):
        if self.defer is not None:
            self.defer.append((eng, fn, tuple(reads), tuple(writes)))
            return
        self._deps(eng, reads, writes)
        c = self.cnt[eng] + 1
        self.q[eng].append(("i", fn))
        self.cnt[eng] = c
        import traceback
        fr = traceback.extract_stack(limit=4)[0]
        self.src.setdefault(eng, {})[c] = f"{fr.name}:{fr.lineno}"
        for b in reads:
            b.r.append((eng, c))
        for b in writes:
            b.w, b.r = (eng, c), []

    def dma(self, queue, fn, reads=(), writes=()):
        iss = "sp" if queue == "ld" else "pool"
        tgt = (list(writes) + list(reads))[0]
        key = ("dl:" if queue == "ld" else "ds:") + tgt.name
        if key not in self.sems:
            self.sems[key] = self.mksem(key)
            self.dtot = getattr(self, "dtot", {})
        self.dtot.setdefault(key, [iss, 0])
        self._deps(iss, reads, writes)
        self.q[iss].append(("D", fn, key))
        self.dtot[key][1] += 16
        c = self.dtot[key][1]
        self.dcnt[queue] += 16
        for b in reads:
            b.r.append((key, c))
        for b in writes:
            b.w, b.r = (key, c), []

    def barrier(self):
        for key, (iss, tot) in getattr(self, "dtot", {}).items():
            for e in ENGS:
                self._wait(e, key, tot)
        arr = dict(self.cnt)
        for e in ENGS:
            for o in ENGS:
                if o != e:
                    self._wait(e, o, arr[o])

    def replay(self, block):
        needed = set()
        for e in ENGS:
            for item in self.q[e]:
                if item[0] == "w" and not item[1].startswith(("dl:", "ds:")):
                    needed.add((item[1], item[2]))
        rank = {e: {} for e in ENGS}
        flags = {e: [] for e in ENGS}
        for e in ENGS:
            c, r = 0, 0
            for item in self.q[e]:
                if item[0] == "i":
                    c += 1
                    f = (e, c) in needed
                    if f:
                        r += 1
                        rank[e][c] = r
                    flags[e].append(f)
        for e in ENGS:
            ops, fl = self.q[e], flags[e]
            def mk(ops=ops, fl=fl, e=e):
                def run(E):
                    ii = 0
                    for item in ops:
                        if item[0] == "w":
                            key, val = item[1], item[2]
                            if key.startswith(("dl:", "ds:")):
                                E.wait_ge(self.sems[key], val)
                            else:
                                E.wait_ge(self.sems[key], rank[key][val])
                        elif item[0] == "D":
                            item[1](E).then_inc(self.sems[item[2]], 16)
                        else:
                            inst = item[1](E)
                            if fl[ii]:
                                inst.then_inc(self.sems[e], 1)
                            ii += 1
                return run
            getattr(block, BLOCKMAP[e])(mk())


def make_consts():
    t = np.arange(128)
    strictlow = (t[None, :] < t[:, None]).astype(np.float32)
    upstrict = (t[None, :] > t[:, None]).astype(np.float32)
    upincl = (t[None, :] >= t[:, None]).astype(np.float32)
    ident = np.eye(128, dtype=np.float32)
    iota8 = np.tile(np.arange(8, dtype=np.float32), (128, 1))
    ghi = np.tile(np.array([0, 0, 0, 0, 1, 1, 1, 1], np.float32), (128, 1))
    glo = 1.0 - ghi
    gdiff = ghi - glo
    iota_mB = iota8 - 1000.0
    cf = np.concatenate(
        [strictlow, np.tile(upstrict, (1, 4)), np.tile(upincl, (1, 4)), ident,
         iota8, ghi, glo, gdiff, iota_mB,
         np.full((128, 1), 1e-6, np.float32), np.full((128, 1), 1e-5, np.float32),
         np.ones((128, 128), np.float32)],
        axis=1).astype(np.float32)
    CF = {"strictlow": (0, 128), "upstrict4": (128, 512), "upincl4": (640, 512),
          "ident": (1152, 128), "iota8": (1280, 8), "ghi": (1288, 8),
          "glo": (1296, 8), "gdiff": (1304, 8), "iota_mB": (1312, 8),
          "eps6": (1320, 1), "eps5": (1321, 1), "onesf": (1322, 128)}
    id4 = np.zeros((128, 512), np.float32)
    for h in range(4):
        id4[:, h * 128:(h + 1) * 128] = ident
    ones_blk = np.ones((128, 128), np.float32)
    hind = np.zeros((128, 16), np.float32)
    for h in range(4):
        hind[:, 4 * h + h] = 1.0
    selrow = np.zeros((128, 12 * 128), np.float32)
    for j in range(12):
        selrow[j, j * 128:(j + 1) * 128] = 1.0
    cb = np.concatenate([ident, id4, ones_blk, hind, selrow], axis=1).astype(ml_dtypes.bfloat16)
    CB = {"ident": (0, 128), "id4": (128, 512), "ones": (640, 128),
          "hind": (768, 16), "selrow": (784, 12 * 128)}
    return cf, CF, cb, CB


def build(n_tiles=16, dbg=(), phases=("A", "B1", "B2")):
    nc = bass.Bass("TRN2", target_bir_lowering=False)
    NT = n_tiles
    dbg = list(dbg)

    x_in = nc.dram_tensor("x", [L, D], F32, kind="ExternalInput")
    Wq_d = nc.dram_tensor("Wq", [D, D], F32, kind="ExternalInput")
    Wk_d = nc.dram_tensor("Wk", [D, D], F32, kind="ExternalInput")
    Wv_d = nc.dram_tensor("Wv", [D, D], F32, kind="ExternalInput")
    Wo_d = nc.dram_tensor("Wo", [D, D], F32, kind="ExternalInput")
    Wsm_d = nc.dram_tensor("Wsm", [D, 12], F32, kind="ExternalInput")
    convq_d = nc.dram_tensor("convq", [D, 4], F32, kind="ExternalInput")
    convk_d = nc.dram_tensor("convk", [D, 4], F32, kind="ExternalInput")
    convv_d = nc.dram_tensor("convv", [D, 4], F32, kind="ExternalInput")
    convm_d = nc.dram_tensor("convm", [4, 4], F32, kind="ExternalInput")
    mixb_d = nc.dram_tensor("mixb", [4, 1], F32, kind="ExternalInput")
    rmsw_d = nc.dram_tensor("rmsw", [2, 128], F32, kind="ExternalInput")
    gate_d = nc.dram_tensor("gate", [NE, D], F32, kind="ExternalInput")
    corrb_d = nc.dram_tensor("corrb", [1, NE], F32, kind="ExternalInput")
    w1_d = nc.dram_tensor("w1", [NE, INTER, LAT], F32, kind="ExternalInput")
    w2_d = nc.dram_tensor("w2", [NE, LAT, INTER], F32, kind="ExternalInput")
    fc1_d = nc.dram_tensor("fc1", [LAT, D], F32, kind="ExternalInput")
    fc2_d = nc.dram_tensor("fc2", [D, LAT], F32, kind="ExternalInput")
    up_d = nc.dram_tensor("up", [SHARED, D], F32, kind="ExternalInput")
    down_d = nc.dram_tensor("down", [D, SHARED], F32, kind="ExternalInput")
    cvec_d = nc.dram_tensor("cvec", [1, D], F32, kind="ExternalInput")
    wmod_d = nc.dram_tensor("wmod", [D, 6 * D], F32, kind="ExternalInput")

    out_d = nc.dram_tensor("out", [L, D], F32, kind="ExternalOutput")
    x1_d = nc.dram_tensor("x1d", [L, D], F32, kind="Internal")
    h2_d = nc.dram_tensor("h2d", [D, L], BF16, kind="Internal")
    acc_d = nc.dram_tensor("accd", [L, D], F32, kind="Internal")

    cf_np, CF, cb_np, CB = make_consts()
    cf_h = nc.inline_tensor(cf_np, name="cf_const")
    cb_h = nc.inline_tensor(cb_np, name="cb_const")

    dbg_d = {nm: nc.dram_tensor("dbg_" + nm, [128, 8192], F32, kind="ExternalOutput")
             for nm in dbg}

    def dump(name, src, reads, c0=0, n=None, p0=128):
        if name in dbg_d:
            dmast(dbg_d[name][0:p0, c0:c0 + n], src, reads=reads)

    p = Prog()
    esg = ExitStack()
    SB = lambda name, shape, dt: esg.enter_context(nc.sbuf_tensor(name, shape, dt))
    for key in ENGS:
        p.sems[key] = esg.enter_context(nc.semaphore("sem_" + key))
    p.mksem = lambda nm: esg.enter_context(nc.semaphore("sem_" + nm.replace(":", "_")))

    BUFS = {}
    def bf(name):
        if name not in BUFS:
            BUFS[name] = Buf(name)
        return BUFS[name]

    def mm(out, lhsT, rhs, st, sp, reads, writes):
        if lhsT.tensor.dtype != rhs.tensor.dtype:
            import traceback
            traceback.print_stack(limit=4)
            raise RuntimeError(f"mm dtype mix {lhsT.tensor.dtype} {rhs.tensor.dtype}")
        p.emit("pe", lambda E, o=out, l=lhsT, r=rhs, a=st, b=sp:
               E.matmul(o, l, r, start=a, stop=b, skip_group_check=True),
               reads=reads, writes=writes)

    def act(out, in_, func, reads, writes, bias=0.0, scale=1.0, acc=None):
        p.emit("act", lambda E, o=out, i=in_, f=func, b=bias, s=scale, a=acc:
               E.activation(o, i, f, bias=b, scale=s, accum_out=a),
               reads=reads, writes=writes)

    def tt(out, a, b, op, reads, writes, eng="dve"):
        p.emit(eng, lambda E, o=out, x=a, y=b, z=op: E.tensor_tensor(o, x, y, z),
               reads=reads, writes=writes)

    def ts(out, a, s1, s2, op0, op1, reads, writes, eng="dve", acc=None):
        p.emit(eng, lambda E, o=out, x=a, u=s1, v=s2, z0=op0, z1=op1, ac=acc:
               (E.tensor_scalar(o, x, u, v, z0) if z1 is None else
                E.tensor_scalar(o, x, u, v, z0, z1, accum_out=ac)),
               reads=reads, writes=writes)

    def stt(out, a, s, b, op0, op1, reads, writes, eng="dve", acc=None):
        p.emit(eng, lambda E, o=out, x=a, u=s, y=b, z0=op0, z1=op1, ac=acc:
               E.scalar_tensor_tensor(o, x, u, y, z0, z1, accum_out=ac),
               reads=reads, writes=writes)

    def red(out, in_, op, reads, writes):
        p.emit("dve", lambda E, o=out, i=in_, z=op: E.tensor_reduce(o, i, AX.X, z),
               reads=reads, writes=writes)

    def cpy(eng, out, in_, reads, writes):
        p.emit(eng, lambda E, o=out, i=in_: E.tensor_copy(o, i), reads=reads, writes=writes)

    def scan(out, d0, d1, init, reads, writes):
        p.emit("dve", lambda E, o=out, a=d0, b=d1, i=init:
               E.tensor_tensor_scan(o, a, b, i, OP.mult, OP.add),
               reads=reads, writes=writes)

    def memset(eng, ap, v, reads, writes):
        p.emit(eng, lambda E, a=ap, c=v: E.memset(a, c), reads=reads, writes=writes)

    def dmald(dst, src, writes, reads=()):
        p.dma("ld", lambda E, d=dst, s=src: E.dma_start(d, s), reads=reads, writes=writes)

    def dmast(dst, src, reads, writes=()):
        p.dma("st", lambda E, d=dst, s=src: E.dma_start(d, s), reads=reads, writes=writes)

    def pump(k):
        n = min(k, len(p.bgq))
        for _ in range(n):
            eng, fn, reads, writes = p.bgq.pop(0)
            p.emit(eng, fn, reads=reads, writes=writes)

    def pump_dve(k):
        n = 0
        while n < k and p.bgq and p.bgq[0][0] == "dve":
            eng, fn, reads, writes = p.bgq.pop(0)
            p.emit(eng, fn, reads=reads, writes=writes)
            n += 1

    def defer_into_bgq(fn, *args):
        p.defer = []
        fn(*args)
        p.bgq.extend(p.defer)
        p.defer = None

    cfs = SB("cf", [128, cf_np.shape[1]], F32)
    cbs = SB("cb", [128, cb_np.shape[1]], BF16)
    def KF(name, p0=0, np_=128, c0=0, nc_=None):
        o, w = CF[name]
        return cfs[p0:p0 + np_, o + c0:o + c0 + (w if nc_ is None else nc_)]
    def KB(name, p0=0, np_=128, c0=0, nc_=None):
        o, w = CB[name]
        return cbs[p0:p0 + np_, o + c0:o + c0 + (w if nc_ is None else nc_)]
    IDB = KB("ident")
    b_cf, b_cb = bf("cf"), bf("cb")
    nc.const_aps.register(F32, 1e-6, cfs[:, CF["eps6"][0]:CF["eps6"][0] + 1])
    nc.const_aps.register(F32, 1e-5, cfs[:, CF["eps5"][0]:CF["eps5"][0] + 1])

    ga_b = SB("ga_b", [128, D], F32); b_gab = bf("ga_b")
    gb_b = SB("gb_b", [128, D], F32); b_gbb = bf("gb_b")
    corrb = SB("corrb_b", [128, NE], F32); b_corrb = bf("corrb")
    mod_cols = SB("mod_cols", [128, 32], F32); b_modc = bf("mod_cols")
    rms_col = SB("rms_col", [128, 2], F32); b_rms = bf("rms_col")
    mixcol = SB("mixcol", [4, 1], F32); b_mixc = bf("mixcol")
    cw = SB("cw", [128, 96], F32); b_cw = bf("cw")
    cwm = SB("cwm", [4, 4], F32); b_cwm = bf("cwm")

    PSS = esg.enter_context(nc.psum_tensor("PSS", [128, 2048], F32))
    PSA = esg.enter_context(nc.psum_tensor("PSA", [128, 512], F32))
    PSB = esg.enter_context(nc.psum_tensor("PSB", [128, 512], F32))
    PSC = esg.enter_context(nc.psum_tensor("PSC", [128, 512], F32))
    PSD = esg.enter_context(nc.psum_tensor("PSD", [128, 512], F32))
    PSX = [PSA, PSB, PSC, PSD]
    b_pss = bf("PSS")
    b_pssq = [bf("PSSq0"), bf("PSSq1"), bf("PSSq2"), bf("PSSq3")]
    b_ps = [bf("PSA"), bf("PSB"), bf("PSC"), bf("PSD")]

    esA = ExitStack()
    SA_ = lambda name, shape, dt: esA.enter_context(nc.sbuf_tensor(name, shape, dt))
    Wq = SA_("Wqb", [128, 8, D], BF16); b_Wq = bf("Wqb")
    Wk = SA_("Wkb", [128, 8, D], BF16); b_Wk = bf("Wkb")
    Wv = SA_("Wvb", [128, 8, D], BF16); b_Wv = bf("Wvb")
    Wo = SA_("Wob", [128, 8, D], BF16); b_Wo = bf("Wob")
    Wsm = SA_("Wsmb", [128, 8, 12], BF16); b_Wsm = bf("Wsmb")

    esI = ExitStack()
    SI = lambda name, shape, dt: esI.enter_context(nc.sbuf_tensor(name, shape, dt))
    row = SI("row", [1, 16 * 1024], F32); b_row = bf("row")
    rowbf = SI("rowbf", [1, 1024], BF16); b_rowbf = bf("rowbf")
    c_cm = SI("c_cm", [128, 8], BF16); b_ccm = bf("c_cm")
    wld = [SI("wld0", [128, 2, 512], F32), SI("wld1", [128, 2, 512], F32)]
    b_wld = [bf("wld0"), bf("wld1")]
    wcs = SI("wcs", [128, 2, 512], BF16); b_wcs = bf("wcs")
    wsmf = SI("wsmf", [128, 8, 12], F32); b_wsmf = bf("wsmf")

    R_MOD, R_CSIL, R_A, R_B, R_C, R_Dr, R_GA, R_GB, R_WK = \
        0, 6144, 7168, 8192, 9216, 10240, 11264, 12288, 13312

    dmald(cfs[:, :], cf_h.ap(), writes=[b_cf])
    dmald(cbs[:, :], cb_h.ap(), writes=[b_cb])
    dmald(corrb[0:1, :], corrb_d.ap(), writes=[b_corrb])
    dmald(mixcol[0:4, 0:1], mixb_d.ap(), writes=[b_mixc])
    p.dma("ld", lambda E: E.dma_start(rms_col[:, 0:2], bass.AP(rmsw_d, 0, [[1, 128], [128, 2]]),
                                      allow_slow_non_contiguous=True), writes=[b_rms])
    for j, cvd in enumerate((convq_d, convk_d, convv_d)):
        dmald(cw[:, j * 32:(j + 1) * 32],
              bass.AP(cvd, 0, [[4, 128], [512, 8], [1, 4]]), writes=[b_cw])
    dmald(cwm[0:4, :], convm_d.ap(), writes=[b_cwm])
    dmald(row[0:1, R_CSIL + 1024:R_CSIL + 2048], cvec_d.ap(), writes=[b_row])
    dmald(wsmf[:, :, :], bass.AP(Wsm_d, 0, [[12, 128], [128 * 12, 8], [1, 12]]),
          writes=[b_wsmf])

    mm(PSA[:, 16:24], KF("onesf", np_=1), corrb[0:1, :], True, True,
       reads=[b_corrb, b_cf], writes=[b_ps[0]])
    act(corrb[:, :], PSA[:, 16:24], AF.Identity, reads=[b_ps[0]], writes=[b_corrb])

    act(row[0:1, R_CSIL:R_CSIL + 1024], row[0:1, R_CSIL + 1024:R_CSIL + 2048], AF.Silu,
        reads=[b_row], writes=[b_row])
    cpy("dve", rowbf[0:1, :], row[0:1, R_CSIL:R_CSIL + 1024], reads=[b_row], writes=[b_rowbf])
    for kc in range(8):
        mm(PSA[:, kc:kc + 1], rowbf[0:1, kc * 128:(kc + 1) * 128],
           KB("ones", np_=1, nc_=1), True, True, reads=[b_rowbf, b_cb], writes=[b_ps[0]])
    act(c_cm[:, :], PSA[:, 0:8], AF.Identity, reads=[b_ps[0]], writes=[b_ccm])

    for ns in range(12):
        for kg in range(4):
            w = wld[kg % 2]; bb = b_wld[kg % 2]
            off = (256 * kg) * (6 * D) + 512 * ns
            dmald(w[:, :, :], bass.AP(wmod_d, off, [[6 * D, 128], [128 * 6 * D, 2], [1, 512]]),
                  writes=[bb])
            cpy("dve", wcs[:, :, :], w[:, :, :], reads=[bb], writes=[b_wcs])
            for j in range(2):
                kc = kg * 2 + j
                mm(PSB[0:1, 0:512], c_cm[:, kc:kc + 1], wcs[:, j, :],
                   kc == 0, kc == 7, reads=[b_wcs, b_ccm], writes=[b_ps[1]])
        act(row[0:1, R_MOD + ns * 512:R_MOD + (ns + 1) * 512], PSB[0:1, 0:512], AF.Identity,
            reads=[b_ps[1]], writes=[b_row])

    for src, dst in ((2 * D, R_GA), (5 * D, R_GB)):
        g = row[0:1, R_MOD + src:R_MOD + src + D]
        ts(row[0:1, R_A:R_A + D], g, 0.25, None, OP.max, None, reads=[b_row], writes=[b_row])
        ts(row[0:1, R_B:R_B + D], g, -0.25, None, OP.min, None, reads=[b_row], writes=[b_row])
        ts(row[0:1, R_C:R_C + D], g, 0.0, None, OP.is_ge, None, reads=[b_row], writes=[b_row])
        tt(row[0:1, R_Dr:R_Dr + D], row[0:1, R_A:R_A + D], row[0:1, R_B:R_B + D],
           OP.subtract, reads=[b_row], writes=[b_row])
        tt(row[0:1, R_A:R_A + D], row[0:1, R_C:R_C + D], row[0:1, R_Dr:R_Dr + D],
           OP.mult, reads=[b_row], writes=[b_row])
        tt(row[0:1, dst:dst + D], row[0:1, R_A:R_A + D], row[0:1, R_B:R_B + D],
           OP.add, reads=[b_row], writes=[b_row])

    for roff, dst, b_dst in ((R_GA, ga_b, b_gab), (R_GB, gb_b, b_gbb)):
        for half in range(2):
            mm(PSB[:, 0:512], KF("onesf", np_=1), row[0:1, roff + half * 512:roff + (half + 1) * 512],
               True, True, reads=[b_row, b_cf], writes=[b_ps[1]])
            act(dst[:, half * 512:(half + 1) * 512], PSB[:, 0:512], AF.Identity,
                reads=[b_ps[1]], writes=[b_dst])

    dump("mod", row[0:1, 0:4096], [b_row], 0, 4096, 1)
    dump("mod2", row[0:1, 4096:6144], [b_row], 0, 2048, 1)
    dump("gab", ga_b[:, :], [b_gab], 0, 1024)
    dump("gbb", gb_b[:, :], [b_gbb], 0, 1024)

    for blk, (src, add1) in enumerate(((1 * D, True), (0 * D, False),
                                      (4 * D, True), (3 * D, False))):
        if add1:
            ts(row[0:1, R_WK:R_WK + D], row[0:1, R_MOD + src:R_MOD + src + D], 1.0, None,
               OP.add, None, reads=[b_row], writes=[b_row])
            srcap = row[0:1, R_WK:R_WK + D]
        else:
            srcap = row[0:1, R_MOD + src:R_MOD + src + D]
        cpy("dve", rowbf[0:1, :], srcap, reads=[b_row], writes=[b_rowbf])
        for kc in range(8):
            mm(PSA[:, kc:kc + 1], rowbf[0:1, kc * 128:(kc + 1) * 128],
               KB("ones", np_=1, nc_=1), True, True, reads=[b_rowbf, b_cb], writes=[b_ps[0]])
        act(mod_cols[:, blk * 8:(blk + 1) * 8], PSA[:, 0:8], AF.Identity,
            reads=[b_ps[0]], writes=[b_modc])

    dump("modc", mod_cols[:, :], [b_modc], 0, 32)
    for Wd, Wt, bw in ((Wq_d, Wq, b_Wq), (Wk_d, Wk, b_Wk), (Wv_d, Wv, b_Wv), (Wo_d, Wo, b_Wo)):
        for kc in range(8):
            w = wld[kc % 2]; bb = b_wld[kc % 2]
            dmald(w[:, :, :], bass.AP(Wd, kc * 128 * D, [[D, 128], [512, 2], [1, 512]]),
                  writes=[bb])
            cpy("dve", Wt[:, kc, 0:512], w[:, 0, :], reads=[bb], writes=[bw])
            cpy("dve", Wt[:, kc, 512:1024], w[:, 1, :], reads=[bb], writes=[bw])
    cpy("dve", Wsm[:, :, :], wsmf[:, :, :], reads=[b_wsmf], writes=[b_Wsm])

    esI.close()

    esA2 = ExitStack()
    SA2_ = lambda name, shape, dt: esA2.enter_context(nc.sbuf_tensor(name, shape, dt))
    xa = [SA2_("xa0", [128, 2, D], F32), SA2_("xa1", [128, 2, D], F32)]
    b_xa = [bf("xa0"), bf("xa1")]
    xntk2 = SA2_("xntk2", [128, 2, 1024], BF16)
    b_xntk2 = [bf("xntkA"), bf("xntkB")]
    stat = SA2_("stat", [128, 8], F32); b_stat = bf("stat")
    xn_cm = SA2_("xn_cm", [128, 8, 256], BF16); b_xn = bf("xn_cm")
    qpre = SA2_("qpre", [128, 8, 259], BF16); b_qpre = bf("qpre")
    kpre = SA2_("kpre", [128, 8, 259], BF16); b_kpre = bf("kpre")
    vpre = SA2_("vpre", [128, 8, 259], BF16); b_vpre = bf("vpre")
    mpre = SA2_("mpre", [4, 259], BF16); b_mpre = bf("mpre")
    cm4 = SA2_("cm4", [4, 256], F32); b_cm4 = bf("cm4")
    mix_s = SA2_("mix_s", [4, 256], F32); b_mixs = bf("mix_s")
    cacc = SA2_("cacc", [128, 8, 256], BF16); b_cacc = bf("cacc")
    q_cm2 = [SA2_("q_cmA", [128, 8, 256], BF16), SA2_("q_cmB", [128, 8, 256], BF16)]
    b_qcm2 = [bf("q_cmA"), bf("q_cmB")]
    k_cm = SA2_("k_cm", [128, 8, 256], BF16); b_kcm = bf("k_cm")
    v_cm = SA2_("v_cm", [128, 8, 256], BF16); b_vcm = bf("v_cm")
    k_n, b_kn = k_cm, b_kcm
    sg4 = SA2_("sg4", [4, 768], F32); b_sg = bf("sg4")
    sgf = SA2_("sgf", [4, 256], F32); b_sgf = bf("sgf")
    sgb = SA2_("sgb", [4, 1408], BF16); b_sgb = bf("sgb")
    b_sgbRR = bf("sgbRR")
    sgt = SA2_("sgt", [128, 2, 4], F32); b_sgt = bf("sgt")
    nbcol = SA2_("nbcol", [128, 2, 4], F32); b_nbcol = bf("nbcol")
    SGB_G, SGB_M, SGB_NB, SGB_RNQ, SGB_RNK, SGB_RR = 0, 256, 512, 768, 1024, 1280
    gamb = SA2_("gamb", [128, 4, 256], BF16); b_gamb = bf("gamb")
    mixbt2 = [SA2_("mixbtA", [128, 4, 256], BF16), SA2_("mixbtB", [128, 4, 256], BF16)]
    b_mixbt2 = [bf("mixbtA"), bf("mixbtB")]
    rnqb = SA2_("rnqb", [128, 4, 256], BF16); b_rnqb = bf("rnqb")
    rnkb = SA2_("rnkb", [128, 4, 256], BF16); b_rnkb = bf("rnkb")
    ema = [SA2_("emaA", [128, 8, 256], BF16), SA2_("emaB", [128, 8, 256], BF16)]
    b_ema = [bf("emaA"), bf("emaB")]
    kntok2 = [SA2_("kntokA", [128, 4, 256], BF16), SA2_("kntokB", [128, 4, 256], BF16)]
    b_kntok2 = [bf("kntokA"), bf("kntokB")]
    nkbtok2 = [SA2_("nkbtokA", [128, 4, 256], BF16), SA2_("nkbtokB", [128, 4, 256], BF16)]
    b_nkbtok2 = [bf("nkbtokA"), bf("nkbtokB")]
    vbtok2 = [SA2_("vbtokA", [128, 4, 256], BF16), SA2_("vbtokB", [128, 4, 256], BF16)]
    b_vbtok2 = [bf("vbtokA"), bf("vbtokB")]
    wnegb = SA2_("wnegb", [128, 8, 128], BF16)
    b_wnegb2 = [bf("wnegbA"), bf("wnegbB")]
    b_wneg_h = lambda c: b_wnegb2[0] if c < 4 else b_wnegb2[1]
    uadjb = SA2_("uadjb", [128, 4, 256], BF16); b_uadjb = bf("uadjb")
    b_bema = b_uadjb
    mscr = SA2_("mscr", [128, 512], BF16); b_mscr = bf("mscr")
    Apb = [SA2_("Ap0", [128, 512], BF16), SA2_("Ap1", [128, 512], BF16),
           SA2_("Ap2", [128, 512], BF16), SA2_("Ap3", [128, 512], BF16)]
    b_Ap = [bf("Ap0"), bf("Ap1"), bf("Ap2"), bf("Ap3")]
    Upb = [SA2_("Up0", [128, 512], BF16), SA2_("Up1", [128, 512], BF16),
           SA2_("Up2", [128, 512], BF16), SA2_("Up3", [128, 512], BF16)]
    b_Up = [bf("Up0"), bf("Up1"), bf("Up2"), bf("Up3")]
    Ptb = [SA2_("Pt0", [128, 512], BF16), SA2_("Pt1", [128, 512], BF16),
           SA2_("Pt2", [128, 512], BF16), SA2_("Pt3", [128, 512], BF16)]
    b_Pt = [bf("Pt0"), bf("Pt1"), bf("Pt2"), bf("Pt3")]

    S_bf = SA2_("S_bf", [128, 8, 256], BF16)
    b_Sbf2 = [bf("S_bfA"), bf("S_bfB")]
    b_Sbf_h = lambda c: b_Sbf2[0] if c < 4 else b_Sbf2[1]
    S_ps = PSS
    b_Sps = b_pss
    bl1 = SA2_("bl1", [128, 8, 128], BF16)
    bl2 = SA2_("bl2", [128, 8, 128], F32)
    b_bl1h = [bf("bl1A"), bf("bl1B")]
    b_bl2h = [bf("bl2A"), bf("bl2B")]
    atnT2 = [SA2_("atnT0", [128, 512], BF16), SA2_("atnT1", [128, 512], BF16)]
    b_atnT2 = [bf("atnT0"), bf("atnT1")]
    onb = wnegb
    rt = nc.virtual_sbuf_tensor("rt", [128, D], F32)
    nc.back_virtual_tensor_with(rt, bl2)


    memset("dve", ema[1][:, :, 255:256], 0.0, reads=[], writes=[b_ema[1]])
    memset("dve", S_bf[:, :, :], 0.0, reads=[], writes=[b_Sbf2[0], b_Sbf2[1]])
    memset("dve", qpre[:, :, 0:3], 0.0, reads=[], writes=[b_qpre])
    memset("dve", kpre[:, :, 0:3], 0.0, reads=[], writes=[b_kpre])
    memset("dve", vpre[:, :, 0:3], 0.0, reads=[], writes=[b_vpre])
    memset("dve", mpre[0:4, 0:3], 0.0, reads=[], writes=[b_mpre])

    def ln_stats(xbuf, b_xbuf, r):
        xk, b_xk = xntk2[:, r, :], b_xntk2[r]
        act(xk, xbuf[:, r, :], AF.Square, reads=[b_xbuf], writes=[b_xk])
        red(stat[:, 0:1], xbuf[:, r, :], OP.add, reads=[b_xbuf], writes=[b_stat])
        red(stat[:, 1:2], xk, OP.add, reads=[b_xk], writes=[b_stat])
        ts(stat[:, 2:3], stat[:, 0:1], 1.0 / D, None, OP.mult, None,
           reads=[b_stat], writes=[b_stat])
        ts(stat[:, 3:4], stat[:, 1:2], 1.0 / D, None, OP.mult, None,
           reads=[b_stat], writes=[b_stat])
        tt(stat[:, 4:5], stat[:, 2:3], stat[:, 2:3], OP.mult, reads=[b_stat], writes=[b_stat])
        tt(stat[:, 5:6], stat[:, 3:4], stat[:, 4:5], OP.subtract,
           reads=[b_stat], writes=[b_stat])
        act(stat[:, 6:7], stat[:, 5:6], AF.Ln, bias=1e-5, reads=[b_stat], writes=[b_stat])
        act(stat[:, 7:8], stat[:, 6:7], AF.Exp, scale=-0.5, reads=[b_stat], writes=[b_stat])
        stt(stat[:, 4:5], stat[:, 2:3], -1.0, stat[:, 7:8], OP.mult, OP.mult,
            reads=[b_stat], writes=[b_stat])
        act(xk, xbuf[:, r, :], AF.Identity, scale=stat[:, 7:8], bias=stat[:, 4:5],
            reads=[b_xbuf, b_stat], writes=[b_xk])

    def ln_trans(r, cols):
        xk, b_xk = xntk2[:, r, :], b_xntk2[r]
        for c in range(8):
            bank = b_ps[c // 4]
            pst = PSX[c // 4]
            mm(pst[:, (c % 4) * 128:(c % 4 + 1) * 128],
               xk[:, c * 128:(c + 1) * 128], IDB, True, True,
               reads=[b_xk, b_cb], writes=[bank])
        for c in range(8):
            pst = PSX[c // 4]
            act(xn_cm[:, c, r * 128:(r + 1) * 128], pst[:, (c % 4) * 128:(c % 4 + 1) * 128],
                AF.Identity, scale=mod_cols[:, cols[0] * 8 + c:cols[0] * 8 + c + 1],
                bias=mod_cols[:, cols[1] * 8 + c:cols[1] * 8 + c + 1],
                reads=[b_ps[c // 4], b_modc], writes=[b_xn])

    def conv_taps(pre, b_pre, cwoff, dstbuf, b_dst):
        for c in range(8):
            ts(dstbuf[:, c, :], pre[:, c, 0:256],
               cw[:, cwoff + c * 4:cwoff + c * 4 + 1], None, OP.mult, None,
               reads=[b_pre, b_cw], writes=[b_dst])
        for k in range(1, 4):
            for c in range(8):
                stt(dstbuf[:, c, :], pre[:, c, k:k + 256],
                    cw[:, cwoff + c * 4 + k:cwoff + c * 4 + k + 1], dstbuf[:, c, :],
                    OP.mult, OP.add, reads=[b_pre, b_cw, b_dst], writes=[b_dst])

    NCH = 2 * NT

    def a_front1_stats(i):
        t0 = i * TA
        xb, b_xb = xa[i % 2], b_xa[i % 2]
        for r in range(2):
            dmald(xb[:, r, :], x_in[t0 + 128 * r:t0 + 128 * r + 128, :], writes=[b_xb])
        for r in range(2):
            ln_stats(xb, b_xb, r)

    def a_front1(i):
        for r in range(2):
            ln_trans(r, (0, 1))
        for (Wt, bw, pre, b_pre, cwoff) in ((Wq, b_Wq, qpre, b_qpre, 0),
                                            (Wk, b_Wk, kpre, b_kpre, 32),
                                            (Wv, b_Wv, vpre, b_vpre, 64)):
            if i > 0:
                cpy("dve", pre[:, :, 0:3], pre[:, :, 256:259], reads=[b_pre], writes=[b_pre])
            for m in range(8):
                slot = m % 4
                ap_ = PSS[:, (slot // 2) * 512 + (slot % 2) * 256:
                          (slot // 2) * 512 + (slot % 2) * 256 + 256]
                bank = b_pssq[slot // 2]
                for kc in range(8):
                    mm(ap_, Wt[:, kc, m * 128:(m + 1) * 128], xn_cm[:, kc, :],
                       kc == 0, kc == 7, reads=[bw, b_xn], writes=[bank])
                act(pre[:, m, 3:259], ap_, AF.Identity, reads=[bank], writes=[b_pre])

        if i > 0:
            cpy("dve", mpre[0:4, 0:3], mpre[0:4, 256:259], reads=[b_mpre], writes=[b_mpre])
        SPR = PSS[0:4, 512:768]
        for g3 in range(3):
            for kc in range(8):
                mm(SPR, Wsm[:, kc, g3 * 4:g3 * 4 + 4], xn_cm[:, kc, :],
                   kc == 0, kc == 7, reads=[b_Wsm, b_xn], writes=[b_pssq[1]])
            if g3 == 0:
                act(sg4[0:4, 0:256], SPR, AF.Sigmoid, reads=[b_pssq[1]], writes=[b_sg])
            elif g3 == 1:
                act(sg4[0:4, 256:512], SPR, AF.Sigmoid, reads=[b_pssq[1]], writes=[b_sg])
            else:
                act(mpre[0:4, 3:259], SPR, AF.Identity, reads=[b_pssq[1]], writes=[b_mpre])


    def a_front2(i):
        t0 = i * TA
        xb, b_xb = xa[i % 2], b_xa[i % 2]
        conv_taps(qpre, b_qpre, 0, cacc, b_cacc)
        q_cm, b_qcm = q_cm2[i % 2], b_qcm2[i % 2]
        mixbt, b_mixbt = mixbt2[i % 2], b_mixbt2[i % 2]
        act(cacc[:, :, :], cacc[:, :, :], AF.Silu, reads=[b_cacc], writes=[b_cacc])
        act(q_cm[:, :, :], cacc[:, :, :], AF.Silu, reads=[b_cacc], writes=[b_qcm])
        conv_taps(kpre, b_kpre, 32, cacc, b_cacc)
        act(cacc[:, :, :], cacc[:, :, :], AF.Silu, reads=[b_cacc], writes=[b_cacc])
        act(k_cm[:, :, :], cacc[:, :, :], AF.Silu, reads=[b_cacc], writes=[b_kcm])
        conv_taps(vpre, b_vpre, 64, cacc, b_cacc)
        act(v_cm[:, :, :], cacc[:, :, :], AF.Silu, reads=[b_cacc], writes=[b_vcm])
        ts(cm4[0:4, :], mpre[0:4, 0:256], cwm[0:4, 0:1], None, OP.mult, None,
           reads=[b_mpre, b_cwm], writes=[b_cm4])
        stt(mix_s[0:4, :], mpre[0:4, 1:257], cwm[0:4, 1:2], cm4[0:4, :], OP.mult, OP.add,
            reads=[b_mpre, b_cwm, b_cm4], writes=[b_mixs])
        stt(cm4[0:4, :], mpre[0:4, 2:258], cwm[0:4, 2:3], mix_s[0:4, :], OP.mult, OP.add,
            reads=[b_mpre, b_cwm, b_mixs], writes=[b_cm4])
        stt(mix_s[0:4, :], mpre[0:4, 3:259], cwm[0:4, 3:4], cm4[0:4, :], OP.mult, OP.add,
            reads=[b_mpre, b_cwm, b_cm4], writes=[b_mixs])
        act(cm4[0:4, :], mix_s[0:4, :], AF.Silu, reads=[b_mixs], writes=[b_cm4])
        act(sg4[0:4, 512:768], cm4[0:4, :], AF.Sigmoid, bias=mixcol[0:4, 0:1],
            reads=[b_cm4, b_mixc], writes=[b_sg])

        cpy("dve", sgb[0:4, 0:512], sg4[0:4, 256:768], reads=[b_sg], writes=[b_sgb])
        ts(sgb[0:4, SGB_NB:SGB_NB + 256], sg4[0:4, 0:256], -1.0, None,
           OP.mult, None, reads=[b_sg], writes=[b_sgb])

    def a_front2b(i):
        t0 = i * TA
        q_cm, b_qcm = q_cm2[i % 2], b_qcm2[i % 2]
        mixbt, b_mixbt = mixbt2[i % 2], b_mixbt2[i % 2]
        for r in range(2):
            mm(PSS[:, r * 4:r * 4 + 4], sg4[0:4, r * 128:(r + 1) * 128],
               KF("ident", np_=4, nc_=4), True, True, reads=[b_sg, b_cf], writes=[b_pssq[0]])
        for r in range(2):
            act(sgt[:, r, :], PSS[:, r * 4:r * 4 + 4], AF.Identity,
                reads=[b_pssq[0]], writes=[b_sgt])
            ts(nbcol[:, r, :], sgt[:, r, 0:4], -1.0, None, OP.mult, None,
               reads=[b_sgt], writes=[b_nbcol])

        for dst, b_dst, seg in ((gamb, b_gamb, SGB_G), (mixbt, b_mixbt, SGB_M)):
            for h in range(4):
                sl_ = PSS[:, (h % 2) * 256:(h % 2) * 256 + 256]
                mm(sl_, KB("selrow", np_=4, c0=h * 128, nc_=128),
                   sgb[0:4, seg:seg + 256], True, True,
                   reads=[b_sgb, b_cb], writes=[b_pssq[0]])
                act(dst[:, h, :], sl_, AF.Identity, reads=[b_pssq[0]], writes=[b_dst])

        for h4 in range(2):
            u4 = bass.AP(uadjb, 0, [[1024, 128], [512, 2], [256, 2], [1, 256]])
            v4 = bass.AP(v_cm, h4 * 1024, [[2048, 128], [512, 2], [256, 2], [1, 256]])
            g4 = bass.AP(gamb, 2 * h4 * 256, [[1024, 128], [256, 2], [0, 2], [1, 256]])
            tt(u4, v4, g4, OP.mult, reads=[b_gamb, b_vcm], writes=[b_bema])
            tt(uadjb[:, :, :], v_cm[:, 4 * h4:4 * h4 + 4, :], uadjb[:, :, :],
               OP.subtract, reads=[b_vcm, b_bema], writes=[b_bema])
            for j in range(4):
                c = 4 * h4 + j
                scan(ema[i % 2][:, c, :], gamb[:, c // 2, :], uadjb[:, j, :],
                     ema[(i + 1) % 2][:, c, 255:256],
                     reads=[b_gamb, b_bema, b_ema[(i + 1) % 2]], writes=[b_ema[i % 2]])

        if i == 0:
            if "xnf" in dbg_d:
                tt(bl1[:, :, :], xn_cm[:, :, 0:128], xn_cm[:, :, 0:128], OP.max,
                   reads=[b_xn], writes=[b_bl1h[0], b_bl1h[1]])
                dump("xnf", bl1[:, :, :], [b_bl1h[0], b_bl1h[1]], 0, 1024)
            dump("xn", xn_cm[:, :, :], [b_xn], 0, 2048)
            dump("qcm", q_cm[:, :, :], [b_qcm], 0, 2048)
            dump("kcm", k_cm[:, :, :], [b_kcm], 0, 2048)
            dump("vcm", v_cm[:, :, :], [b_vcm], 0, 2048)
            dump("sg", sg4[0:4, 0:768], [b_sg], 0, 768, 4)
            dump("ema", ema[0][:, :, :], [b_ema[0]], 0, 2048)
        tt(cacc[:, :, :], q_cm[:, :, :], q_cm[:, :, :], OP.mult, reads=[b_qcm], writes=[b_cacc])
        for kc in range(8):
            mm(PSS[0:4, 512:768], KB("hind", c0=4 * (kc // 2), nc_=4), cacc[:, kc, :],
               kc == 0, kc == 7, reads=[b_cacc, b_cb], writes=[b_pssq[1]])
        act(sgf[0:4, :], PSS[0:4, 512:768], AF.Ln, bias=1e-6, reads=[b_pssq[1]], writes=[b_sgf])
        act(sgb[0:4, SGB_RNQ:SGB_RNQ + 256], sgf[0:4, :], AF.Exp, scale=-0.5,
            reads=[b_sgf], writes=[b_sgb])
        tt(cacc[:, :, :], k_cm[:, :, :], k_cm[:, :, :], OP.mult, reads=[b_kcm], writes=[b_cacc])
        for kc in range(8):
            mm(PSS[0:4, 512:768], KB("hind", c0=4 * (kc // 2), nc_=4), cacc[:, kc, :],
               kc == 0, kc == 7, reads=[b_cacc, b_cb], writes=[b_pssq[1]])
        act(sgf[0:4, :], PSS[0:4, 512:768], AF.Ln, bias=1e-6, reads=[b_pssq[1]], writes=[b_sgf])
        act(sgb[0:4, SGB_RNK:SGB_RNK + 256], sgf[0:4, :], AF.Exp, scale=-0.5,
            reads=[b_sgf], writes=[b_sgb])
        for dst, b_dst, seg in ((rnqb, b_rnqb, SGB_RNQ), (rnkb, b_rnkb, SGB_RNK)):
            for h in range(4):
                sl_ = PSS[:, (h % 2) * 256:(h % 2) * 256 + 256]
                mm(sl_, KB("selrow", np_=4, c0=h * 128, nc_=128), sgb[0:4, seg:seg + 256],
                   True, True, reads=[b_sgb, b_cb], writes=[b_pssq[0]])
                act(dst[:, h, :], sl_, AF.Identity, reads=[b_pssq[0]], writes=[b_dst])
        q4d = bass.AP(q_cm, 0, [[2048, 128], [512, 4], [256, 2], [1, 256]])
        k4d = bass.AP(k_cm, 0, [[2048, 128], [512, 4], [256, 2], [1, 256]])
        rq4 = bass.AP(rnqb, 0, [[1024, 128], [256, 4], [0, 2], [1, 256]])
        rk4 = bass.AP(rnkb, 0, [[1024, 128], [256, 4], [0, 2], [1, 256]])
        tt(q4d, q4d, rq4, OP.mult, reads=[b_qcm, b_rnqb], writes=[b_qcm])
        tt(k4d, k4d, rk4, OP.mult, reads=[b_kcm, b_rnkb], writes=[b_kcm])


    def a_delta(i):
        t0 = i * TA
        xb, b_xb = xa[i % 2], b_xa[i % 2]
        q_n, b_qn = q_cm2[i % 2], b_qcm2[i % 2]
        QB = lambda q: (PSS[:, q * 512:(q + 1) * 512], b_pssq[q])
        BANKS = [
            {"kt": [(PSA, b_ps[0]), (PSB, b_ps[1])], "vt": [(PSC, b_ps[2]), (PSD, b_ps[3])],
             "ga": (PSA, b_ps[0]), "gq": (PSB, b_ps[1]),
             "ub": [(PSC, b_ps[2]), (PSD, b_ps[3])],
             "a2": (PSC, b_ps[2]), "u2": (PSD, b_ps[3]), "pt": (PSB, b_ps[1])},
            {"kt": [QB(0), QB(1)], "vt": [QB(2), QB(3)],
             "ga": QB(0), "gq": QB(1), "ub": [QB(2), QB(3)],
             "a2": QB(2), "u2": QB(3), "pt": QB(1)},
        ]
        atnT_ap = lambda r, h: atnT2[r][:, h * 128:(h + 1) * 128]
        Tt_fin = [None, None]

        def d_trans(r):
            toff = r * 128
            bk = BANKS[r]
            for src, b_src, key in ((k_n, b_kn, "kt"), (v_cm, b_vcm, "vt")):
                for h in range(4):
                    for ci in range(2):
                        pst, bank = bk[key][h // 2]
                        mm(pst[:, ((h % 2) * 2 + ci) * 128:((h % 2) * 2 + ci) * 128 + 128],
                           src[:, 2 * h + ci, toff:toff + 128], IDB, True, True,
                           reads=[b_src, b_cb], writes=[bank])
            for hp in range(2):
                pst, bank = bk["kt"][hp]
                act(kntok2[r][:, 2 * hp:2 * hp + 2, :], pst[:, 0:512], AF.Identity,
                    reads=[bank], writes=[b_kntok2[r]])
            for h in range(4):
                pst, bank = bk["kt"][h // 2]
                sl = pst[:, (h % 2) * 256:(h % 2) * 256 + 256]
                act(nkbtok2[r][:, h, :], sl, AF.Identity, scale=nbcol[:, r, h:h + 1],
                    reads=[bank, b_nbcol], writes=[b_nkbtok2[r]])
            for h in range(4):
                pst, bank = bk["vt"][h // 2]
                sl = pst[:, (h % 2) * 256:(h % 2) * 256 + 256]
                act(vbtok2[r][:, h, :], sl, AF.Identity, scale=sgt[:, r, h:h + 1],
                    reads=[bank, b_sgt], writes=[b_vbtok2[r]])

        def d_gram(r):
            toff = r * 128
            bk = BANKS[r]
            ga, b_ga = bk["ga"]
            gq, b_gq = bk["gq"]
            for h in range(4):
                for ci in range(2):
                    mm(ga[:, h * 128:(h + 1) * 128], k_n[:, 2 * h + ci, toff:toff + 128],
                       k_n[:, 2 * h + ci, toff:toff + 128], ci == 0, ci == 1,
                       reads=[b_kn], writes=[b_ga])
            for h in range(4):
                for ci in range(2):
                    mm(gq[:, h * 128:(h + 1) * 128], k_n[:, 2 * h + ci, toff:toff + 128],
                       q_n[:, 2 * h + ci, toff:toff + 128], ci == 0, ci == 1,
                       reads=[b_kn, b_qn], writes=[b_gq])

        def d_mask(r):
            toff = r * 128
            bk = BANKS[r]
            ga, b_ga = bk["ga"]
            gq, b_gq = bk["gq"]
            for h in range(4):
                stt(Apb[2 * r][:, h * 128:(h + 1) * 128], ga[:, h * 128:(h + 1) * 128],
                    nbcol[:, r, h:h + 1], KF("strictlow"), OP.mult, OP.mult,
                    reads=[b_ga, b_nbcol, b_cf], writes=[b_Ap[2 * r]])
            for h in range(4):
                pst, bank = bk["ub"][h % 2]
                mm(pst[:, (h // 2) * 256:(h // 2) * 256 + 128],
                   KB("selrow", np_=4, c0=h * 128, nc_=128),
                   sgb[0:4, SGB_NB + toff:SGB_NB + toff + 128], True, True,
                   reads=[b_sgb, b_cb], writes=[bank])
            for h in range(4):
                pst, bank = bk["ub"][h % 2]
                tt(mscr[:, h * 128:(h + 1) * 128], pst[:, (h // 2) * 256:(h // 2) * 256 + 128],
                   KF("upstrict4", c0=h * 128, nc_=128), OP.mult,
                   reads=[bank, b_cf], writes=[b_mscr])
            tt(Upb[2 * r][:, :], ga[:, 0:512], mscr[:, :], OP.mult,
               reads=[b_ga, b_mscr], writes=[b_Up[2 * r]])
            tt(atnT2[r][:, :], gq[:, 0:512], KF("upincl4"), OP.mult,
               reads=[b_gq, b_cf], writes=[b_atnT2[r]])

        def d_tt(r, j, pa):
            bk = BANKS[r]
            na = 1 - pa
            a2, b_a2 = bk["a2"]
            u2, b_u2 = bk["u2"]
            ptp, b_ptp = bk["pt"]
            iA = lambda p: 2 * r + p
            for h in range(4):
                mm(a2[:, h * 128:(h + 1) * 128], Upb[iA(pa)][:, h * 128:(h + 1) * 128],
                   Apb[iA(pa)][:, h * 128:(h + 1) * 128], True, True,
                   reads=[b_Up[iA(pa)], b_Ap[iA(pa)]], writes=[b_a2])
            act(Apb[iA(na)][:, :], a2[:, 0:512], AF.Identity, reads=[b_a2],
                writes=[b_Ap[iA(na)]])
            if j < 6:
                for h in range(4):
                    mm(u2[:, h * 128:(h + 1) * 128], Apb[iA(pa)][:, h * 128:(h + 1) * 128],
                       Upb[iA(pa)][:, h * 128:(h + 1) * 128], True, True,
                       reads=[b_Up[iA(pa)], b_Ap[iA(pa)]], writes=[b_u2])
                act(Upb[iA(na)][:, :], u2[:, 0:512], AF.Identity, reads=[b_u2],
                    writes=[b_Up[iA(na)]])
            mm(ptp[:, 0:512], IDB, Ptb[iA(pa)][:, :], True, False,
               reads=[b_Pt[iA(pa)], b_cb], writes=[b_ptp])
            for h in range(4):
                mm(ptp[:, h * 128:(h + 1) * 128], Apb[iA(na)][:, h * 128:(h + 1) * 128],
                   Ptb[iA(pa)][:, h * 128:(h + 1) * 128], False, True,
                   reads=[b_Ap[iA(na)], b_Pt[iA(pa)]], writes=[b_ptp])
            act(Ptb[iA(na)][:, :], ptp[:, 0:512], AF.Identity, reads=[b_ptp],
                writes=[b_Pt[iA(na)]])
            return na

        for r in range(2):
            d_trans(r)
        for r in range(2):
            d_gram(r)
        for r in range(2):
            d_mask(r)
        for r in range(2):
            tt(Ptb[2 * r][:, :], Upb[2 * r][:, :], KB("id4"), OP.add,
               reads=[b_Up[2 * r], b_cb], writes=[b_Pt[2 * r]])
        pas = [0, 0]
        for j in range(1, 7):
            for r in range(2):
                pas[r] = d_tt(r, j, pas[r])
        for r in range(2):
            Tt_fin[r] = (Ptb[2 * r + pas[r]], b_Pt[2 * r + pas[r]])

        return Tt_fin

    def a_scans(i, Tt_fin):
        t0 = i * TA
        xb, b_xb = xa[i % 2], b_xa[i % 2]
        q_n, b_qn = q_cm2[i % 2], b_qcm2[i % 2]
        mixbt, b_mixbt = mixbt2[i % 2], b_mixbt2[i % 2]
        atnT_ap = lambda r, h: atnT2[r][:, h * 128:(h + 1) * 128]
        for r in range(2):
            cc = 2 * i + r
            toff = r * 128
            Tt, b_Tt = Tt_fin[r]
            kntok, b_kntok = kntok2[r], b_kntok2[r]
            nkbtok, b_nkbtok = nkbtok2[r], b_nkbtok2[r]
            vbtok, b_vbtok = vbtok2[r], b_vbtok2[r]
            for h in range(4):
                for ci in range(2):
                    pst, bank = PSX[h // 2], b_ps[h // 2]
                    mm(pst[:, ((h % 2) * 2 + ci) * 128:((h % 2) * 2 + ci) * 128 + 128],
                       nkbtok[:, h, ci * 128:ci * 128 + 128], Tt[:, h * 128:(h + 1) * 128],
                       True, True, reads=[b_nkbtok, b_Tt], writes=[bank])
            for hp in range(2):
                act(wnegb[:, 4 * hp:4 * hp + 4, :], PSX[hp][:, 0:512], AF.Identity,
                    reads=[b_ps[hp]], writes=[b_wnegb2[hp]])
            pump_dve(16)
            for h in range(4):
                pst, bank = PSX[2 + h // 2], b_ps[2 + h // 2]
                mm(pst[:, (h % 2) * 256:(h % 2) * 256 + 256], Tt[:, h * 128:(h + 1) * 128],
                   vbtok[:, h, :], True, False, reads=[b_Tt, b_vbtok], writes=[bank])
                for ci in range(2):
                    mm(pst[:, (h % 2) * 256:(h % 2) * 256 + 256], wnegb[:, 2 * h + ci, :],
                       S_bf[:, 2 * h + ci, :], False, ci == 1,
                       reads=[b_wneg_h(2 * h + ci), b_Sbf_h(2 * h + ci)], writes=[bank])
            for hp in range(2):
                act(uadjb[:, 2 * hp:2 * hp + 2, :], PSX[2 + hp][:, 0:512], AF.Identity,
                    reads=[b_ps[2 + hp]], writes=[b_uadjb])
            pump_dve(16)
            for h in range(4):
                pst, bank = PSX[h // 2], b_ps[h // 2]
                for m2 in range(2):
                    for ci in range(2):
                        mm(pst[:, ((h % 2) * 2 + m2) * 128:((h % 2) * 2 + m2) * 128 + 128],
                           S_bf[:, 2 * h + ci, m2 * 128:m2 * 128 + 128],
                           q_n[:, 2 * h + ci, toff:toff + 128], ci == 0, False,
                           reads=[b_Sbf_h(2 * h + ci), b_qn], writes=[bank])
                    mm(pst[:, ((h % 2) * 2 + m2) * 128:((h % 2) * 2 + m2) * 128 + 128],
                       uadjb[:, h, m2 * 128:m2 * 128 + 128], atnT_ap(r, h),
                       False, True, reads=[b_uadjb, b_atnT2[r]], writes=[bank])
            pump_dve(2)
            for wave in range(2):
                for k4 in range(4):
                    hc = wave * 4 + k4
                    h, ci = hc // 2, hc % 2
                    sl = S_ps[:, (2 + k4 // 2) * 512 + (k4 % 2) * 256:
                              (2 + k4 // 2) * 512 + (k4 % 2) * 256 + 256]
                    bq = b_pssq[2 + k4 // 2]
                    if cc > 0:
                        mm(sl, IDB, S_bf[:, hc, :], True, False,
                           reads=[b_cb, b_Sbf_h(hc)], writes=[bq])
                    mm(sl, kntok[:, h, ci * 128:ci * 128 + 128], uadjb[:, h, :],
                       cc == 0, True, reads=[b_kntok, b_uadjb], writes=[bq])
                for q2 in range(2):
                    act(S_bf[:, wave * 4 + 2 * q2:wave * 4 + 2 * q2 + 2, :],
                        S_ps[:, (2 + q2) * 512:(2 + q2) * 512 + 512],
                        AF.Identity, reads=[b_pssq[2 + q2]], writes=[b_Sbf2[wave]])
            pump(6)
            for hb in range(2):
                tt(bl1[:, 4 * hb:4 * hb + 4, :],
                   ema[i % 2][:, 4 * hb:4 * hb + 4, toff:toff + 128], PSX[hb][:, 0:512],
                   OP.subtract, reads=[b_ema[i % 2], b_ps[hb]], writes=[b_bl1h[hb]])
                bl1h4 = bass.AP(bl1, hb * 512, [[1024, 128], [256, 2], [128, 2], [1, 128]])
                bl2h4 = bass.AP(bl2, hb * 512, [[1024, 128], [256, 2], [128, 2], [1, 128]])
                mx4 = bass.AP(mixbt, 2 * hb * 256 + toff,
                              [[1024, 128], [256, 2], [0, 2], [1, 128]])
                tt(bl2h4, bl1h4, mx4, OP.mult,
                   reads=[b_bl1h[hb], b_mixbt], writes=[b_bl2h[hb]])
                tt(bl1[:, 4 * hb:4 * hb + 4, :], PSX[hb][:, 0:512],
                   bl2[:, 4 * hb:4 * hb + 4, :], OP.add,
                   reads=[b_ps[hb], b_bl2h[hb]], writes=[b_bl1h[hb]])
                tt(wnegb[:, 4 * hb:4 * hb + 4, :], bl1[:, 4 * hb:4 * hb + 4, :],
                   bl1[:, 4 * hb:4 * hb + 4, :], OP.mult,
                   reads=[b_bl1h[hb]], writes=[b_wnegb2[hb]])
            for c in range(8):
                mm(PSC[0:4, 0:128], KB("hind", c0=4 * (c // 2), nc_=4),
                   wnegb[:, c, :], c == 0, c == 7,
                   reads=[b_wneg_h(c), b_cb], writes=[b_ps[2]])
            act(sgf[0:4, 0:128], PSC[0:4, 0:128], AF.Ln, scale=1.0 / 256, bias=1e-5,
                reads=[b_ps[2]], writes=[b_sgf])
            act(sgb[0:4, SGB_RR:SGB_RR + 128], sgf[0:4, 0:128], AF.Exp, scale=-0.5,
                reads=[b_sgf], writes=[b_sgbRR])
            for h in range(4):
                mm(PSD[:, h * 128:(h + 1) * 128], KB("selrow", np_=4, c0=h * 128, nc_=128),
                   sgb[0:4, SGB_RR:SGB_RR + 128], True, True, reads=[b_sgbRR, b_cb],
                   writes=[b_ps[3]])
            for c in range(8):
                stt(onb[:, c, :], bl1[:, c, :], rms_col[:, c % 2:c % 2 + 1],
                    PSD[:, (c // 2) * 128:(c // 2) * 128 + 128], OP.mult, OP.mult,
                    reads=[b_bl1h[c // 4], b_rms, b_ps[3]], writes=[b_wneg_h(c)])
            pump(8)
            for kc in range(8):
                mm(PSA[:, 0:512], onb[:, kc, :], Wo[:, kc, 0:512], kc == 0, kc == 7,
                   reads=[b_wneg_h(kc), b_Wo], writes=[b_ps[0]])
            for kc in range(8):
                mm(PSB[:, 0:512], onb[:, kc, :], Wo[:, kc, 512:1024], kc == 0, kc == 7,
                   reads=[b_wneg_h(kc), b_Wo], writes=[b_ps[1]])
            tt(rt[:, 0:512], PSA[:, 0:512], ga_b[:, 0:512], OP.mult,
               reads=[b_ps[0], b_gab], writes=[b_bl2h[0]])
            tt(rt[:, 512:1024], PSB[:, 0:512], ga_b[:, 512:1024], OP.mult,
               reads=[b_ps[1], b_gab], writes=[b_bl2h[1]])
            tt(xb[:, r, :], xb[:, r, :], rt[:, :], OP.add,
               reads=[b_xb, b_bl2h[0], b_bl2h[1]], writes=[b_xb])
            dmast(x1_d[t0 + r * 128:t0 + r * 128 + 128, :], xb[:, r, :],
                  reads=[b_xb])
            if "B2" not in phases:
                dmast(out_d[t0 + r * 128:t0 + r * 128 + 128, :], xb[:, r, :],
                      reads=[b_xb])
            if i == 0:
                dump("x1", xb[:, r, :], [b_xb], r * 1024, 1024)
            pump(8)
        pump(len(p.bgq))


    a_front1_stats(0)
    a_front1(0)
    a_front2(0)
    for i in range(NT):
        a_front2b(i)
        if i + 1 < NT:
            a_front1_stats(i + 1)
        tf = a_delta(i)
        if i + 1 < NT:
            a_front1(i + 1)
            defer_into_bgq(a_front2, i + 1)
        a_scans(i, tf)

    p.barrier()
    esA2.close()
    esA.close()

    if "B1" not in phases:
        with nc.Block() as block:
            p.replay(block)
        esg.close()
        return nc, p

    esB = ExitStack()
    SBb = lambda name, shape, dt: esB.enter_context(nc.sbuf_tensor(name, shape, dt))
    upT = SBb("upT", [128, 8, SHARED], BF16); b_upT = bf("upT")
    dnT = SBb("dnT", [128, 32, D], BF16); b_dnT = bf("dnT")
    ldb = [SBb("ldb0", [128, 1024], F32), SBb("ldb1", [128, 1024], F32)]
    b_ldb = [bf("ldb0"), bf("ldb1")]
    cstb = SBb("cstb", [128, 1024], BF16); b_cstb = bf("cstb")
    x1a = [SBb("x1a0", [128, 2, D], F32), SBb("x1a1", [128, 2, D], F32)]
    b_x1a = [bf("x1a0"), bf("x1a1")]
    xnt2 = SBb("xnt2", [128, 2, 1024], BF16)
    b_xnt2h = [bf("xnt2A"), bf("xnt2B")]
    stat2 = SBb("stat2", [128, 8], F32); b_stat2 = bf("stat2")
    h2c2 = [SBb("h2cA", [128, 8, 256], BF16), SBb("h2cB", [128, 8, 256], BF16)]
    b_h2c2 = [bf("h2cA"), bf("h2cB")]
    r1b = SBb("r1b", [128, 256], BF16); b_r1b = bf("r1b")
    rsh = SBb("rsh", [128, 32, 256], BF16); b_rsh = bf("rsh")
    rt2 = ldb[0]
    b_rt2 = b_ldb[0]
    accs1t = SBb("accs0", [128, D], F32)
    accs = [accs1t, accs1t]
    b_accs = [bf("accs0"), bf("accs0")]

    for rc in range(32):
        w, bb = ldb[rc % 2], b_ldb[rc % 2]
        dmald(w[:, :], up_d[rc * 128:(rc + 1) * 128, :], writes=[bb])
        cpy("dve", cstb[:, :], w[:, :], reads=[bb], writes=[b_cstb])
        for c in range(8):
            pst, bank = PSX[c // 4], b_ps[c // 4]
            mm(pst[:, (c % 4) * 128:(c % 4) * 128 + 128], cstb[:, c * 128:(c + 1) * 128],
               IDB, True, True, reads=[b_cstb, b_cb], writes=[bank])
        for c in range(8):
            pst = PSX[c // 4]
            act(upT[:, c, rc * 128:(rc + 1) * 128], pst[:, (c % 4) * 128:(c % 4) * 128 + 128],
                AF.Identity, reads=[b_ps[c // 4]], writes=[b_upT])
    for rc in range(8):
        for q4 in range(4):
            w, bb = ldb[q4 % 2], b_ldb[q4 % 2]
            dmald(w[:, :], bass.AP(down_d, rc * 128 * SHARED + q4 * 1024,
                                   [[SHARED, 128], [1, 1024]]), writes=[bb])
            cpy("dve", cstb[:, :], w[:, :], reads=[bb], writes=[b_cstb])
            for c in range(8):
                pst, bank = PSX[c // 4], b_ps[c // 4]
                mm(pst[:, (c % 4) * 128:(c % 4) * 128 + 128], cstb[:, c * 128:(c + 1) * 128],
                   IDB, True, True, reads=[b_cstb, b_cb], writes=[bank])
            for c in range(8):
                pst = PSX[c // 4]
                act(dnT[:, q4 * 8 + c, rc * 128:(rc + 1) * 128],
                    pst[:, (c % 4) * 128:(c % 4) * 128 + 128], AF.Identity,
                    reads=[b_ps[c // 4]], writes=[b_dnT])

    def ln2_stats(xbuf, b_xbuf, r):
        xk, b_xk = xnt2[:, r, :], b_xnt2h[r]
        act(xk, xbuf[:, r, :], AF.Square, reads=[b_xbuf], writes=[b_xk])
        red(stat2[:, 0:1], xbuf[:, r, :], OP.add, reads=[b_xbuf], writes=[b_stat2])
        red(stat2[:, 1:2], xk, OP.add, reads=[b_xk], writes=[b_stat2])
        ts(stat2[:, 2:3], stat2[:, 0:1], 1.0 / D, None, OP.mult, None,
           reads=[b_stat2], writes=[b_stat2])
        ts(stat2[:, 3:4], stat2[:, 1:2], 1.0 / D, None, OP.mult, None,
           reads=[b_stat2], writes=[b_stat2])
        tt(stat2[:, 4:5], stat2[:, 2:3], stat2[:, 2:3], OP.mult,
           reads=[b_stat2], writes=[b_stat2])
        tt(stat2[:, 5:6], stat2[:, 3:4], stat2[:, 4:5], OP.subtract,
           reads=[b_stat2], writes=[b_stat2])
        act(stat2[:, 6:7], stat2[:, 5:6], AF.Ln, bias=1e-5, reads=[b_stat2], writes=[b_stat2])
        act(stat2[:, 7:8], stat2[:, 6:7], AF.Exp, scale=-0.5, reads=[b_stat2], writes=[b_stat2])
        stt(stat2[:, 4:5], stat2[:, 2:3], -1.0, stat2[:, 7:8], OP.mult, OP.mult,
            reads=[b_stat2], writes=[b_stat2])
        act(xk, xbuf[:, r, :], AF.Identity, scale=stat2[:, 7:8], bias=stat2[:, 4:5],
            reads=[b_xbuf, b_stat2], writes=[b_xk])

    def ln2_trans(r, hc, b_hc):
        xk, b_xk = xnt2[:, r, :], b_xnt2h[r]
        for c in range(8):
            pst, bank = PSX[c // 4], b_ps[c // 4]
            mm(pst[:, (c % 4) * 128:(c % 4 + 1) * 128],
               xk[:, c * 128:(c + 1) * 128], IDB, True, True,
               reads=[b_xk, b_cb], writes=[bank])
        for c in range(8):
            pst = PSX[c // 4]
            act(hc[:, c, r * 128:(r + 1) * 128], pst[:, (c % 4) * 128:(c % 4 + 1) * 128],
                AF.Identity, scale=mod_cols[:, 16 + c:16 + c + 1],
                bias=mod_cols[:, 24 + c:24 + c + 1],
                reads=[b_ps[c // 4], b_modc], writes=[b_hc])

    def b1_front(i):
        t0 = i * TA
        xb, b_xb = x1a[i % 2], b_x1a[i % 2]
        for r in range(2):
            dmald(xb[:, r, :], x1_d[t0 + 128 * r:t0 + 128 * r + 128, :], writes=[b_xb])
        for r in range(2):
            ln2_stats(xb, b_xb, r)

    def b1_trans(i):
        hc, b_hc = h2c2[i % 2], b_h2c2[i % 2]
        for r in range(2):
            ln2_trans(r, hc, b_hc)

    b1_front(0)
    b1_trans(0)
    for i in range(NT):
        t0 = i * TA
        xb, b_xb = x1a[i % 2], b_x1a[i % 2]
        hc, b_hc = h2c2[i % 2], b_h2c2[i % 2]
        dmast(bass.AP(h2_d, t0, [[L, 128], [128 * L, 8], [1, 256]]), hc[:, :, :],
              reads=[b_hc])
        if i == 0:
            dump("h2", hc[:, :, :], [b_hc], 0, 2048)
        for ic in range(32):
            slot = ic % 4
            pst, bank = PSX[slot], b_ps[slot]
            for kc in range(8):
                mm(pst[:, 0:256], upT[:, kc, ic * 128:(ic + 1) * 128], hc[:, kc, :],
                   kc == 0, kc == 7, reads=[b_upT, b_hc], writes=[bank])
            act(r1b[:, :], pst[:, 0:256], AF.Relu, reads=[bank], writes=[b_r1b])
            tt(rsh[:, ic, :], r1b[:, :], r1b[:, :], OP.mult,
               reads=[b_r1b], writes=[b_rsh])
        if i + 1 < NT:
            b1_front(i + 1)
        for t2 in range(2):
            for dh in range(2):
                for ic in range(32):
                    mm(PSS[:, t2 * 1024 + dh * 512:t2 * 1024 + dh * 512 + 512],
                       rsh[:, ic, t2 * 128:t2 * 128 + 128], dnT[:, ic, dh * 512:dh * 512 + 512],
                       ic == 0, ic == 31, reads=[b_rsh, b_dnT],
                       writes=[b_pssq[t2 * 2 + dh]])
        if i + 1 < NT:
            b1_trans(i + 1)
        for r in range(2):
            tt(rt2[:, 0:512], PSS[:, r * 1024:r * 1024 + 512], gb_b[:, 0:512], OP.mult,
               reads=[b_pssq[2 * r], b_gbb], writes=[b_rt2])
            tt(rt2[:, 512:1024], PSS[:, r * 1024 + 512:r * 1024 + 1024], gb_b[:, 512:1024],
               OP.mult, reads=[b_pssq[2 * r + 1], b_gbb], writes=[b_rt2])
            tt(accs[r][:, :], xb[:, r, :], rt2[:, :], OP.add,
               reads=[b_xb, b_rt2], writes=[b_accs[r]])
            dmast(acc_d[t0 + r * 128:t0 + r * 128 + 128, :], accs[r][:, :],
                  reads=[b_accs[r]])
            if i == 0:
                dump("acc", accs[r][:, :], [b_accs[r]], r * 1024, 1024)

    p.barrier()
    esB.close()

    if "B2" not in phases:
        with nc.Block() as block:
            p.replay(block)
        esg.close()
        return nc, p

    esC = ExitStack()
    SC = lambda name, shape, dt: esC.enter_context(nc.sbuf_tensor(name, shape, dt))
    gT = SC("gT", [128, 8, 8], BF16); b_gT = bf("gT")
    f1T = SC("f1T", [128, 8, 512], BF16); b_f1T = bf("f1T")
    f2T = SC("f2T", [128, 4, 1024], BF16); b_f2T = bf("f2T")
    w1T = SC("w1T", [128, 32, 512], BF16); b_w1T = bf("w1T")
    w2T = SC("w2T", [128, 32, 512], BF16); b_w2T = bf("w2T")
    ldc = [SC("ldc0", [128, 1024], F32), SC("ldc1", [128, 1024], F32)]
    b_ldc = [bf("ldc0"), bf("ldc1")]
    cstc = SC("cstc", [128, 1024], BF16); b_cstc = bf("cstc")
    g8 = SC("g8", [8, 1024], BF16); b_g8 = bf("g8")
    h2t = [SC("h2t0", [128, 8, 512], BF16), SC("h2t1", [128, 8, 512], BF16)]
    b_h2t = [bf("h2t0"), bf("h2t1")]
    acct = [SC("acct0", [128, 4, D], F32), SC("acct1", [128, 4, D], F32)]
    b_acct = [bf("acct0"), bf("acct1")]
    sS = SC("sS", [128, 4, 8], F32); b_sS = bf("sS")
    we = SC("we", [128, 4, 8], F32); b_we = bf("we")
    rs4 = SC("rs4", [128, 4, 32], F32); b_rs = bf("rs4")
    latb = SC("latb", [128, 4, 512], BF16)
    b_latb4 = [bf("latb0"), bf("latb1"), bf("latb2"), bf("latb3")]
    r1q = SC("r1q", [128, 4, 512], BF16); b_r1q = bf("r1q")
    r2q = SC("r2q", [128, 4, 512], BF16); b_r2q = bf("r2q")
    combq = [SC("comb0", [128, 4, 512], F32), SC("comb1", [128, 4, 512], F32)]
    b_combq = [bf("comb0"), bf("comb1")]
    combbf = SC("combbf", [128, 4, 512], BF16); b_combbf = bf("combbf")
    combcm = SC("combcm", [128, 4, 512], BF16); b_combcm = bf("combcm")
    rt3 = SC("rt3", [128, D], F32); b_rt3 = bf("rt3")
    outs = [SC("outs0", [128, D], F32), SC("outs1", [128, D], F32)]
    b_outs = [bf("outs0"), bf("outs1")]

    dmald(ldc[0][0:8, :], gate_d.ap(), writes=[b_ldc[0]])
    cpy("dve", g8[0:8, :], ldc[0][0:8, :], reads=[b_ldc[0]], writes=[b_g8])
    for c in range(8):
        mm(PSA[:, (c % 4) * 128:(c % 4) * 128 + 8],
           g8[0:8, c * 128:(c + 1) * 128], KB("ident", np_=8, nc_=8), True, True,
           reads=[b_g8, b_cb], writes=[b_ps[0]])
        act(gT[:, c, :], PSA[:, (c % 4) * 128:(c % 4) * 128 + 8], AF.Identity,
            reads=[b_ps[0]], writes=[b_gT])
    for rc in range(4):
        w, bb = ldc[rc % 2], b_ldc[rc % 2]
        dmald(w[:, :], fc1_d[rc * 128:(rc + 1) * 128, :], writes=[bb])
        cpy("dve", cstc[:, :], w[:, :], reads=[bb], writes=[b_cstc])
        for c in range(8):
            pst, bank = PSX[c // 4], b_ps[c // 4]
            mm(pst[:, (c % 4) * 128:(c % 4) * 128 + 128], cstc[:, c * 128:(c + 1) * 128],
               IDB, True, True, reads=[b_cstc, b_cb], writes=[bank])
        for c in range(8):
            act(f1T[:, c, rc * 128:(rc + 1) * 128],
                PSX[c // 4][:, (c % 4) * 128:(c % 4) * 128 + 128], AF.Identity,
                reads=[b_ps[c // 4]], writes=[b_f1T])
    for rc in range(8):
        w, bb = ldc[rc % 2], b_ldc[rc % 2]
        dmald(w[:, 0:512], fc2_d[rc * 128:(rc + 1) * 128, :], writes=[bb])
        cpy("dve", cstc[:, 0:512], w[:, 0:512], reads=[bb], writes=[b_cstc])
        for c in range(4):
            pst, bank = PSX[c], b_ps[c]
            mm(pst[:, 0:128], cstc[:, c * 128:(c + 1) * 128], IDB, True, True,
               reads=[b_cstc, b_cb], writes=[bank])
        for c in range(4):
            act(f2T[:, c, rc * 128:(rc + 1) * 128], PSX[c][:, 0:128], AF.Identity,
                reads=[b_ps[c]], writes=[b_f2T])
    for wd, wt, bwt in ((w1_d, w1T, b_w1T), (w2_d, w2T, b_w2T)):
        for e in range(NE):
            for rc in range(4):
                w, bb = ldc[rc % 2], b_ldc[rc % 2]
                dmald(w[:, 0:512], bass.AP(wd, e * INTER * LAT + rc * 128 * 512,
                                           [[512, 128], [1, 512]]), writes=[bb])
                cpy("dve", cstc[:, 0:512], w[:, 0:512], reads=[bb], writes=[b_cstc])
                for c in range(4):
                    pst, bank = PSX[c], b_ps[c]
                    mm(pst[:, 0:128], cstc[:, c * 128:(c + 1) * 128], IDB, True, True,
                       reads=[b_cstc, b_cb], writes=[bank])
                for c in range(4):
                    act(wt[:, e * 4 + c, rc * 128:(rc + 1) * 128], PSX[c][:, 0:128], AF.Identity,
                        reads=[b_ps[c]], writes=[bwt])

    NT2 = NT // 2

    def b2_loads(i):
        t0 = i * 512
        dmald(h2t[i % 2][:, :, :], bass.AP(h2_d, t0, [[L, 128], [128 * L, 8], [1, 512]]),
              writes=[b_h2t[i % 2]])
        dmald(acct[i % 2][:, :, :], bass.AP(acc_d, t0 * D, [[D, 128], [128 * D, 4], [1, 1024]]),
              writes=[b_acct[i % 2]])

    b2_loads(0)
    for i in range(NT2):
        t0 = i * 512
        ht, b_ht = h2t[i % 2], b_h2t[i % 2]
        at, b_at = acct[i % 2], b_acct[i % 2]
        if i + 1 < NT2:
            b2_loads(i + 1)
        for tc in range(4):
            for kc in range(8):
                mm(PSA[0:128, tc * 128:tc * 128 + 8], ht[:, kc, tc * 128:(tc + 1) * 128],
                   gT[:, kc, :], kc == 0, kc == 7, reads=[b_ht, b_gT], writes=[b_ps[0]])
        act(sS[:, :, :], bass.AP(PSA, 0, [[512, 128], [128, 4], [1, 8]]), AF.Sigmoid,
            reads=[b_ps[0]], writes=[b_sS])
        R = lambda a, b2: rs4[:, :, a:b2]
        Rb = lambda a, w: bass.AP(rs4, a, [[128, 128], [32, 4], [0, w]])
        KFb = lambda name, w: bass.AP(cfs, CF[name][0], [[cf_np.shape[1], 128], [0, 4], [1, w]])
        CRb = bass.AP(corrb, 0, [[NE, 128], [0, 4], [1, NE]])
        tt(R(0, 8), sS[:, :, :], CRb, OP.add, reads=[b_sS, b_corrb], writes=[b_rs])
        if i == 0:
            dump("ra", rs4[:, 0, 0:24], [b_rs], 0, 24)
        for g in range(2):
            sl = R(4 * g, 4 * g + 4)
            red(R(16 + g, 17 + g), sl, OP.max, reads=[b_rs], writes=[b_rs])
            tt(R(8, 12), sl, Rb(16 + g, 4), OP.is_equal, reads=[b_rs], writes=[b_rs])
            tt(R(12, 16), R(8, 12), KFb("iota_mB", 4), OP.mult,
               reads=[b_rs, b_cf], writes=[b_rs])
            ts(R(12, 16), R(12, 16), 1000.0, None, OP.add, None, reads=[b_rs], writes=[b_rs])
            red(R(18, 19), R(12, 16), OP.min, reads=[b_rs], writes=[b_rs])
            tt(R(8, 12), KFb("iota8", 4), Rb(18, 4), OP.is_equal,
               reads=[b_cf, b_rs], writes=[b_rs])
            stt(R(12, 16), R(8, 12), -1e9, sl, OP.mult, OP.add, reads=[b_rs], writes=[b_rs])
            red(R(19, 20), R(12, 16), OP.max, reads=[b_rs], writes=[b_rs])
            tt(R(16 + g, 17 + g), R(16 + g, 17 + g), R(19, 20), OP.add,
               reads=[b_rs], writes=[b_rs])
        tt(R(20, 21), R(17, 18), R(16, 17), OP.is_gt, reads=[b_rs], writes=[b_rs])
        if i == 0:
            dump("rb", rs4[:, 0, 0:24], [b_rs], 0, 24)
        tt(R(8, 16), KFb("gdiff", 8), Rb(20, 8), OP.mult, reads=[b_cf, b_rs], writes=[b_rs])
        tt(R(8, 16), R(8, 16), KFb("glo", 8), OP.add, reads=[b_rs, b_cf], writes=[b_rs])
        tt(R(8, 16), R(0, 8), R(8, 16), OP.mult, reads=[b_rs], writes=[b_rs])
        if i == 0:
            dump("rc", rs4[:, 0, 0:24], [b_rs], 0, 24)
        red(R(16, 17), R(8, 16), OP.max, reads=[b_rs], writes=[b_rs])
        tt(R(0, 8), R(8, 16), Rb(16, 8), OP.is_equal, reads=[b_rs], writes=[b_rs])
        tt(R(0, 8), R(0, 8), KFb("iota_mB", 8), OP.mult, reads=[b_rs, b_cf], writes=[b_rs])
        ts(R(0, 8), R(0, 8), 1000.0, None, OP.add, None, reads=[b_rs], writes=[b_rs])
        red(R(17, 18), R(0, 8), OP.min, reads=[b_rs], writes=[b_rs])
        tt(R(0, 8), KFb("iota8", 8), Rb(17, 8), OP.is_equal,
           reads=[b_cf, b_rs], writes=[b_rs])
        if i == 0:
            dump("rd", rs4[:, 0, 0:24], [b_rs], 0, 24)
        stt(R(8, 16), R(0, 8), -1e9, R(8, 16), OP.mult, OP.add,
            reads=[b_rs], writes=[b_rs])
        if i == 0:
            dump("re", rs4[:, 0, 0:24], [b_rs], 0, 24)
        red(R(18, 19), R(8, 16), OP.max, reads=[b_rs], writes=[b_rs])
        tt(R(8, 16), R(8, 16), Rb(18, 8), OP.is_equal, reads=[b_rs], writes=[b_rs])
        if i == 0:
            dump("rf", rs4[:, 0, 0:24], [b_rs], 0, 24)
        tt(R(8, 16), R(8, 16), KFb("iota_mB", 8), OP.mult, reads=[b_rs, b_cf], writes=[b_rs])
        ts(R(8, 16), R(8, 16), 1000.0, None, OP.add, None, reads=[b_rs], writes=[b_rs])
        red(R(19, 20), R(8, 16), OP.min, reads=[b_rs], writes=[b_rs])
        tt(R(8, 16), KFb("iota8", 8), Rb(19, 8), OP.is_equal,
           reads=[b_cf, b_rs], writes=[b_rs])
        if i == 0:
            dump("rg", rs4[:, 0, 0:24], [b_rs], 0, 24)
        tt(R(24, 32), R(0, 8), sS[:, :, :], OP.mult, reads=[b_rs, b_sS], writes=[b_rs])
        red(R(20, 21), R(24, 32), OP.add, reads=[b_rs], writes=[b_rs])
        tt(R(24, 32), R(8, 16), sS[:, :, :], OP.mult, reads=[b_rs, b_sS], writes=[b_rs])
        red(R(21, 22), R(24, 32), OP.add, reads=[b_rs], writes=[b_rs])
        stt(R(22, 23), R(20, 21), 1e-20, R(21, 22), OP.add, OP.add,
            reads=[b_rs], writes=[b_rs])
        p.emit("dve", lambda E, o=R(23, 24), i_=R(22, 23): E.reciprocal(o, i_),
               reads=[b_rs], writes=[b_rs])
        tt(R(20, 21), R(20, 21), R(23, 24), OP.mult, reads=[b_rs], writes=[b_rs])
        tt(R(21, 22), R(21, 22), R(23, 24), OP.mult, reads=[b_rs], writes=[b_rs])
        tt(R(0, 8), R(0, 8), Rb(20, 8), OP.mult, reads=[b_rs], writes=[b_rs])
        tt(R(24, 32), R(8, 16), Rb(21, 8), OP.mult, reads=[b_rs], writes=[b_rs])
        tt(we[:, :, :], R(24, 32), R(0, 8), OP.add, reads=[b_rs], writes=[b_we])
        if i == 0:
            dump("we", we[:, :, :], [b_we], 0, 32)
        for lc in range(4):
            for kc in range(8):
                mm(PSS[:, lc * 512:(lc + 1) * 512], f1T[:, kc, lc * 128:(lc + 1) * 128],
                   ht[:, kc, :], kc == 0, kc == 7, reads=[b_f1T, b_ht],
                   writes=[b_pssq[lc]])
        for lc in range(4):
            act(latb[:, lc, :], PSS[:, lc * 512:(lc + 1) * 512], AF.Identity,
                reads=[b_pssq[lc]], writes=[b_latb4[lc]])
        def w1mms(e):
            for ic in range(4):
                for lc in range(4):
                    mm(PSS[:, ic * 512:(ic + 1) * 512], w1T[:, e * 4 + lc, ic * 128:(ic + 1) * 128],
                       latb[:, lc, :], lc == 0, lc == 3, reads=[b_w1T, b_latb4[lc]],
                       writes=[b_pssq[ic]])
        w1mms(0)
        for e in range(NE):
            for ic in range(4):
                act(r1q[:, ic, :], PSS[:, ic * 512:(ic + 1) * 512], AF.Relu,
                    reads=[b_pssq[ic]], writes=[b_r1q])
                tt(r2q[:, ic, :], r1q[:, ic, :], r1q[:, ic, :], OP.mult,
                   reads=[b_r1q], writes=[b_r2q])
            if e + 1 < NE:
                w1mms(e + 1)
            for tc in range(4):
                for ic in range(4):
                    mm(PSX[tc][:, 0:512], r2q[:, ic, tc * 128:(tc + 1) * 128],
                       w2T[:, e * 4 + ic, :], ic == 0, ic == 3,
                       reads=[b_r2q, b_w2T], writes=[b_ps[tc]])
            for tc in range(4):
                if e == 0:
                    ts(combq[0][:, tc, :], PSX[tc][:, 0:512], we[:, tc, 0:1], None,
                       OP.mult, None, reads=[b_ps[tc], b_we], writes=[b_combq[0]])
                else:
                    stt(combq[e % 2][:, tc, :], PSX[tc][:, 0:512], we[:, tc, e:e + 1],
                        combq[(e + 1) % 2][:, tc, :], OP.mult, OP.add,
                        reads=[b_ps[tc], b_we, b_combq[(e + 1) % 2]], writes=[b_combq[e % 2]])
        fin = (NE - 1) % 2
        for tc in range(4):
            act(combbf[:, tc, :], combq[fin][:, tc, :], AF.Identity,
                reads=[b_combq[fin]], writes=[b_combbf])
        for tc in range(4):
            for lc in range(4):
                pst, bank = PSX[lc], b_ps[lc]
                mm(pst[:, tc * 128:(tc + 1) * 128], combbf[:, tc, lc * 128:(lc + 1) * 128],
                   IDB, True, True, reads=[b_combbf, b_cb], writes=[bank])
        for lc in range(4):
            act(combcm[:, lc, :], PSX[lc][:, 0:512], AF.Identity,
                reads=[b_ps[lc]], writes=[b_combcm])
        for tc in range(4):
            pa_, pb_ = PSX[2 * (tc % 2)], PSX[2 * (tc % 2) + 1]
            ba_, bb_ = b_ps[2 * (tc % 2)], b_ps[2 * (tc % 2) + 1]
            for lc in range(4):
                mm(pa_[:, 0:512], combcm[:, lc, tc * 128:(tc + 1) * 128],
                   f2T[:, lc, 0:512], lc == 0, lc == 3, reads=[b_combcm, b_f2T], writes=[ba_])
            for lc in range(4):
                mm(pb_[:, 0:512], combcm[:, lc, tc * 128:(tc + 1) * 128],
                   f2T[:, lc, 512:1024], lc == 0, lc == 3, reads=[b_combcm, b_f2T], writes=[bb_])
            tt(rt3[:, 0:512], pa_[:, 0:512], gb_b[:, 0:512], OP.mult,
               reads=[ba_, b_gbb], writes=[b_rt3])
            tt(rt3[:, 512:1024], pb_[:, 0:512], gb_b[:, 512:1024], OP.mult,
               reads=[bb_, b_gbb], writes=[b_rt3])
            tt(outs[tc % 2][:, :], at[:, tc, :], rt3[:, :], OP.add,
               reads=[b_at, b_rt3], writes=[b_outs[tc % 2]])
            dmast(out_d[t0 + tc * 128:t0 + tc * 128 + 128, :], outs[tc % 2][:, :],
                  reads=[b_outs[tc % 2]])

    p.barrier()
    esC.close()

    with nc.Block() as block:
        p.replay(block)
    esg.close()
    return nc, p


def _prep_inputs(inputs, core):
    g = lambda k: np.ascontiguousarray(inputs[k])
    d = {
        "x": np.ascontiguousarray(inputs["L0_attn_hidden_states"][core]),
        "Wq": g("L0_attn_Wq"), "Wk": g("L0_attn_Wk"), "Wv": g("L0_attn_Wv"),
        "Wo": g("L0_attn_Wo"),
        "Wsm": np.ascontiguousarray(np.concatenate(
            [inputs["L0_attn_Wb"], inputs["L0_attn_Wdec"], inputs["L0_attn_Wmix"]], axis=1)),
        "convq": g("L0_attn_conv_q"), "convk": g("L0_attn_conv_k"),
        "convv": g("L0_attn_conv_v"), "convm": g("L0_attn_conv_mix"),
        "mixb": np.ascontiguousarray(inputs["L0_attn_mix_bias"].reshape(4, 1)),
        "rmsw": np.ascontiguousarray(inputs["L0_attn_rms_w"].reshape(2, 128)),
        "gate": g("L0_ffn_gate"),
        "corrb": np.ascontiguousarray(inputs["L0_ffn_corr_bias"].reshape(1, 8)),
        "w1": g("L0_ffn_w1"), "w2": g("L0_ffn_w2"),
        "fc1": g("L0_ffn_fc1_w"), "fc2": g("L0_ffn_fc2_w"),
        "up": g("L0_ffn_up_w"), "down": g("L0_ffn_down_w"),
        "cvec": np.ascontiguousarray(inputs["L0_c"][core].reshape(1, 1024)),
        "wmod": g("L0_wmod"),
    }
    return d


_CACHE = {}


def kernel(**inputs):
    key = "full"
    if key not in _CACHE:
        _CACHE[key] = build(n_tiles=16)
    nc, _ = _CACHE[key]
    in_maps = [_prep_inputs(inputs, c) for c in range(8)]
    res = run_bass_kernel_spmd(nc, in_maps, core_ids=list(range(8)))
    return np.stack([np.asarray(r["out"]) for r in res.results], axis=0)


if __name__ == "__main__":
    nc, p = build(n_tiles=2)
    print({e: p.cnt[e] for e in ENGS}, p.dcnt)


# version 21 (final)
# speedup vs baseline: 1.0307x; 1.0307x over previous
import numpy as np
import ml_dtypes
import concourse.bass as bass
import concourse.mybir as mybir
from concourse.bass_utils import run_bass_kernel_spmd
from contextlib import ExitStack

F32 = mybir.dt.float32
BF16 = mybir.dt.bfloat16
AF = mybir.ActivationFunctionType
OP = mybir.AluOpType
AX = mybir.AxisListType

L, D, H, DK = 4096, 1024, 4, 256
TA = 256
NE, LAT, INTER, SHARED = 8, 512, 512, 4096

ENGS = ("pe", "act", "dve", "sp", "pool")
BLOCKMAP = {"pe": "tensor", "act": "scalar", "dve": "vector", "sp": "sync", "pool": "gpsimd"}


class Buf:
    __slots__ = ("w", "r", "name")
    def __init__(self, name=""):
        self.w, self.r, self.name = None, [], name


class Prog:
    HAZ = ("dve", "act", "pool")

    def __init__(self):
        self.q = {e: [] for e in ENGS}
        self.cnt = {e: 0 for e in ENGS}
        self.dcnt = {"ld": 0, "st": 0}
        self.hw = {}
        self.sems = {}
        self.pend = {e: set() for e in ENGS}
        self.src = {}
        self.nnop = 0
        self.defer = None
        self.bgq = []

    def _wait(self, eng, key, val):
        if key.startswith(("dl:", "ds:")):
            val = max(val, self.dtot[key][1])
            if val <= self.hw.get((eng, key), 0):
                return
            self.hw[(eng, key)] = val
            self.q[eng].append(("w", key, val))
            return
        if eng == "pe" and key == "pe":
            return
        if val <= self.hw.get((eng, key), 0):
            return
        self.hw[(eng, key)] = val
        self.q[eng].append(("w", key, val))

    def _deps(self, eng, reads, writes):
        for b in reads:
            if b.w:
                self._wait(eng, b.w[0], b.w[1])
        for b in writes:
            if b.w:
                self._wait(eng, b.w[0], b.w[1])
            for (re, rc) in b.r:
                self._wait(eng, re, rc)

    def emit(self, eng, fn, reads=(), writes=()):
        if self.defer is not None:
            self.defer.append((eng, fn, tuple(reads), tuple(writes)))
            return
        self._deps(eng, reads, writes)
        c = self.cnt[eng] + 1
        self.q[eng].append(("i", fn))
        self.cnt[eng] = c
        import traceback
        fr = traceback.extract_stack(limit=4)[0]
        self.src.setdefault(eng, {})[c] = f"{fr.name}:{fr.lineno}"
        for b in reads:
            b.r.append((eng, c))
        for b in writes:
            b.w, b.r = (eng, c), []

    def dma(self, queue, fn, reads=(), writes=()):
        iss = "sp" if queue == "ld" else "pool"
        tgt = (list(writes) + list(reads))[0]
        key = ("dl:" if queue == "ld" else "ds:") + tgt.name
        if key not in self.sems:
            self.sems[key] = self.mksem(key)
            self.dtot = getattr(self, "dtot", {})
        self.dtot.setdefault(key, [iss, 0])
        self._deps(iss, reads, writes)
        self.q[iss].append(("D", fn, key))
        self.dtot[key][1] += 16
        c = self.dtot[key][1]
        self.dcnt[queue] += 16
        for b in reads:
            b.r.append((key, c))
        for b in writes:
            b.w, b.r = (key, c), []

    def barrier(self):
        for key, (iss, tot) in getattr(self, "dtot", {}).items():
            for e in ENGS:
                self._wait(e, key, tot)
        arr = dict(self.cnt)
        for e in ENGS:
            for o in ENGS:
                if o != e:
                    self._wait(e, o, arr[o])

    def replay(self, block):
        needed = set()
        for e in ENGS:
            for item in self.q[e]:
                if item[0] == "w" and not item[1].startswith(("dl:", "ds:")):
                    needed.add((item[1], item[2]))
        rank = {e: {} for e in ENGS}
        flags = {e: [] for e in ENGS}
        for e in ENGS:
            c, r = 0, 0
            for item in self.q[e]:
                if item[0] == "i":
                    c += 1
                    f = (e, c) in needed
                    if f:
                        r += 1
                        rank[e][c] = r
                    flags[e].append(f)
        for e in ENGS:
            ops, fl = self.q[e], flags[e]
            def mk(ops=ops, fl=fl, e=e):
                def run(E):
                    ii = 0
                    for item in ops:
                        if item[0] == "w":
                            key, val = item[1], item[2]
                            if key.startswith(("dl:", "ds:")):
                                E.wait_ge(self.sems[key], val)
                            else:
                                E.wait_ge(self.sems[key], rank[key][val])
                        elif item[0] == "D":
                            item[1](E).then_inc(self.sems[item[2]], 16)
                        else:
                            inst = item[1](E)
                            if fl[ii]:
                                inst.then_inc(self.sems[e], 1)
                            ii += 1
                return run
            getattr(block, BLOCKMAP[e])(mk())


def make_consts():
    t = np.arange(128)
    strictlow = (t[None, :] < t[:, None]).astype(np.float32)
    upstrict = (t[None, :] > t[:, None]).astype(np.float32)
    upincl = (t[None, :] >= t[:, None]).astype(np.float32)
    ident = np.eye(128, dtype=np.float32)
    iota8 = np.tile(np.arange(8, dtype=np.float32), (128, 1))
    ghi = np.tile(np.array([0, 0, 0, 0, 1, 1, 1, 1], np.float32), (128, 1))
    glo = 1.0 - ghi
    gdiff = ghi - glo
    iota_mB = iota8 - 1000.0
    cf = np.concatenate(
        [strictlow, np.tile(upstrict, (1, 4)), np.tile(upincl, (1, 4)), ident,
         iota8, ghi, glo, gdiff, iota_mB,
         np.full((128, 1), 1e-6, np.float32), np.full((128, 1), 1e-5, np.float32),
         np.ones((128, 128), np.float32)],
        axis=1).astype(np.float32)
    CF = {"strictlow": (0, 128), "upstrict4": (128, 512), "upincl4": (640, 512),
          "ident": (1152, 128), "iota8": (1280, 8), "ghi": (1288, 8),
          "glo": (1296, 8), "gdiff": (1304, 8), "iota_mB": (1312, 8),
          "eps6": (1320, 1), "eps5": (1321, 1), "onesf": (1322, 128)}
    id4 = np.zeros((128, 512), np.float32)
    for h in range(4):
        id4[:, h * 128:(h + 1) * 128] = ident
    ones_blk = np.ones((128, 128), np.float32)
    hind = np.zeros((128, 16), np.float32)
    for h in range(4):
        hind[:, 4 * h + h] = 1.0
    selrow = np.zeros((128, 12 * 128), np.float32)
    for j in range(12):
        selrow[j, j * 128:(j + 1) * 128] = 1.0
    cb = np.concatenate([ident, id4, ones_blk, hind, selrow], axis=1).astype(ml_dtypes.bfloat16)
    CB = {"ident": (0, 128), "id4": (128, 512), "ones": (640, 128),
          "hind": (768, 16), "selrow": (784, 12 * 128)}
    return cf, CF, cb, CB


def build(n_tiles=16, dbg=(), phases=("A", "B1", "B2")):
    nc = bass.Bass("TRN2", target_bir_lowering=False)
    NT = n_tiles
    dbg = list(dbg)

    x_in = nc.dram_tensor("x", [L, D], F32, kind="ExternalInput")
    Wq_d = nc.dram_tensor("Wq", [D, D], F32, kind="ExternalInput")
    Wk_d = nc.dram_tensor("Wk", [D, D], F32, kind="ExternalInput")
    Wv_d = nc.dram_tensor("Wv", [D, D], F32, kind="ExternalInput")
    Wo_d = nc.dram_tensor("Wo", [D, D], F32, kind="ExternalInput")
    Wsm_d = nc.dram_tensor("Wsm", [D, 12], F32, kind="ExternalInput")
    convq_d = nc.dram_tensor("convq", [D, 4], F32, kind="ExternalInput")
    convk_d = nc.dram_tensor("convk", [D, 4], F32, kind="ExternalInput")
    convv_d = nc.dram_tensor("convv", [D, 4], F32, kind="ExternalInput")
    convm_d = nc.dram_tensor("convm", [4, 4], F32, kind="ExternalInput")
    mixb_d = nc.dram_tensor("mixb", [4, 1], F32, kind="ExternalInput")
    rmsw_d = nc.dram_tensor("rmsw", [2, 128], F32, kind="ExternalInput")
    gate_d = nc.dram_tensor("gate", [NE, D], F32, kind="ExternalInput")
    corrb_d = nc.dram_tensor("corrb", [1, NE], F32, kind="ExternalInput")
    w1_d = nc.dram_tensor("w1", [NE, INTER, LAT], F32, kind="ExternalInput")
    w2_d = nc.dram_tensor("w2", [NE, LAT, INTER], F32, kind="ExternalInput")
    fc1_d = nc.dram_tensor("fc1", [LAT, D], F32, kind="ExternalInput")
    fc2_d = nc.dram_tensor("fc2", [D, LAT], F32, kind="ExternalInput")
    up_d = nc.dram_tensor("up", [SHARED, D], F32, kind="ExternalInput")
    down_d = nc.dram_tensor("down", [D, SHARED], F32, kind="ExternalInput")
    cvec_d = nc.dram_tensor("cvec", [1, D], F32, kind="ExternalInput")
    wmod_d = nc.dram_tensor("wmod", [D, 6 * D], F32, kind="ExternalInput")

    out_d = nc.dram_tensor("out", [L, D], F32, kind="ExternalOutput")
    x1_d = nc.dram_tensor("x1d", [L, D], F32, kind="Internal")
    h2_d = nc.dram_tensor("h2d", [D, L], BF16, kind="Internal")
    acc_d = nc.dram_tensor("accd", [L, D], F32, kind="Internal")

    cf_np, CF, cb_np, CB = make_consts()
    cf_h = nc.inline_tensor(cf_np, name="cf_const")
    cb_h = nc.inline_tensor(cb_np, name="cb_const")

    dbg_d = {nm: nc.dram_tensor("dbg_" + nm, [128, 8192], F32, kind="ExternalOutput")
             for nm in dbg}

    def dump(name, src, reads, c0=0, n=None, p0=128):
        if name in dbg_d:
            dmast(dbg_d[name][0:p0, c0:c0 + n], src, reads=reads)

    p = Prog()
    esg = ExitStack()
    SB = lambda name, shape, dt: esg.enter_context(nc.sbuf_tensor(name, shape, dt))
    for key in ENGS:
        p.sems[key] = esg.enter_context(nc.semaphore("sem_" + key))
    p.mksem = lambda nm: esg.enter_context(nc.semaphore("sem_" + nm.replace(":", "_")))

    BUFS = {}
    def bf(name):
        if name not in BUFS:
            BUFS[name] = Buf(name)
        return BUFS[name]

    def mm(out, lhsT, rhs, st, sp, reads, writes):
        if lhsT.tensor.dtype != rhs.tensor.dtype:
            import traceback
            traceback.print_stack(limit=4)
            raise RuntimeError(f"mm dtype mix {lhsT.tensor.dtype} {rhs.tensor.dtype}")
        p.emit("pe", lambda E, o=out, l=lhsT, r=rhs, a=st, b=sp:
               E.matmul(o, l, r, start=a, stop=b, skip_group_check=True),
               reads=reads, writes=writes)

    def act(out, in_, func, reads, writes, bias=0.0, scale=1.0, acc=None):
        p.emit("act", lambda E, o=out, i=in_, f=func, b=bias, s=scale, a=acc:
               E.activation(o, i, f, bias=b, scale=s, accum_out=a),
               reads=reads, writes=writes)

    def tt(out, a, b, op, reads, writes, eng="dve"):
        p.emit(eng, lambda E, o=out, x=a, y=b, z=op: E.tensor_tensor(o, x, y, z),
               reads=reads, writes=writes)

    def ts(out, a, s1, s2, op0, op1, reads, writes, eng="dve", acc=None):
        p.emit(eng, lambda E, o=out, x=a, u=s1, v=s2, z0=op0, z1=op1, ac=acc:
               (E.tensor_scalar(o, x, u, v, z0) if z1 is None else
                E.tensor_scalar(o, x, u, v, z0, z1, accum_out=ac)),
               reads=reads, writes=writes)

    def stt(out, a, s, b, op0, op1, reads, writes, eng="dve", acc=None):
        p.emit(eng, lambda E, o=out, x=a, u=s, y=b, z0=op0, z1=op1, ac=acc:
               E.scalar_tensor_tensor(o, x, u, y, z0, z1, accum_out=ac),
               reads=reads, writes=writes)

    def red(out, in_, op, reads, writes):
        p.emit("dve", lambda E, o=out, i=in_, z=op: E.tensor_reduce(o, i, AX.X, z),
               reads=reads, writes=writes)

    def cpy(eng, out, in_, reads, writes):
        p.emit(eng, lambda E, o=out, i=in_: E.tensor_copy(o, i), reads=reads, writes=writes)

    def scan(out, d0, d1, init, reads, writes):
        p.emit("dve", lambda E, o=out, a=d0, b=d1, i=init:
               E.tensor_tensor_scan(o, a, b, i, OP.mult, OP.add),
               reads=reads, writes=writes)

    def memset(eng, ap, v, reads, writes):
        p.emit(eng, lambda E, a=ap, c=v: E.memset(a, c), reads=reads, writes=writes)

    def dmald(dst, src, writes, reads=()):
        p.dma("ld", lambda E, d=dst, s=src: E.dma_start(d, s), reads=reads, writes=writes)

    def dmast(dst, src, reads, writes=()):
        p.dma("st", lambda E, d=dst, s=src: E.dma_start(d, s), reads=reads, writes=writes)

    def pump(k):
        n = min(k, len(p.bgq))
        for _ in range(n):
            eng, fn, reads, writes = p.bgq.pop(0)
            p.emit(eng, fn, reads=reads, writes=writes)

    def pump_dve(k):
        n = 0
        while n < k and p.bgq and p.bgq[0][0] == "dve":
            eng, fn, reads, writes = p.bgq.pop(0)
            p.emit(eng, fn, reads=reads, writes=writes)
            n += 1

    def defer_into_bgq(fn, *args):
        p.defer = []
        fn(*args)
        p.bgq.extend(p.defer)
        p.defer = None

    cfs = SB("cf", [128, cf_np.shape[1]], F32)
    cbs = SB("cb", [128, cb_np.shape[1]], BF16)
    def KF(name, p0=0, np_=128, c0=0, nc_=None):
        o, w = CF[name]
        return cfs[p0:p0 + np_, o + c0:o + c0 + (w if nc_ is None else nc_)]
    def KB(name, p0=0, np_=128, c0=0, nc_=None):
        o, w = CB[name]
        return cbs[p0:p0 + np_, o + c0:o + c0 + (w if nc_ is None else nc_)]
    IDB = KB("ident")
    b_cf, b_cb = bf("cf"), bf("cb")
    nc.const_aps.register(F32, 1e-6, cfs[:, CF["eps6"][0]:CF["eps6"][0] + 1])
    nc.const_aps.register(F32, 1e-5, cfs[:, CF["eps5"][0]:CF["eps5"][0] + 1])

    ga_b = SB("ga_b", [128, D], F32); b_gab = bf("ga_b")
    gb_b = SB("gb_b", [128, D], F32); b_gbb = bf("gb_b")
    corrb = SB("corrb_b", [128, NE], F32); b_corrb = bf("corrb")
    mod_cols = SB("mod_cols", [128, 32], F32); b_modc = bf("mod_cols")
    rms_col = SB("rms_col", [128, 2], F32); b_rms = bf("rms_col")
    mixcol = SB("mixcol", [4, 1], F32); b_mixc = bf("mixcol")
    cw = SB("cw", [128, 96], F32); b_cw = bf("cw")
    cwm = SB("cwm", [4, 4], F32); b_cwm = bf("cwm")

    PSS = esg.enter_context(nc.psum_tensor("PSS", [128, 2048], F32))
    PSA = esg.enter_context(nc.psum_tensor("PSA", [128, 512], F32))
    PSB = esg.enter_context(nc.psum_tensor("PSB", [128, 512], F32))
    PSC = esg.enter_context(nc.psum_tensor("PSC", [128, 512], F32))
    PSD = esg.enter_context(nc.psum_tensor("PSD", [128, 512], F32))
    PSX = [PSA, PSB, PSC, PSD]
    b_pss = bf("PSS")
    b_pssq = [bf("PSSq0"), bf("PSSq1"), bf("PSSq2"), bf("PSSq3")]
    b_ps = [bf("PSA"), bf("PSB"), bf("PSC"), bf("PSD")]

    esA = ExitStack()
    SA_ = lambda name, shape, dt: esA.enter_context(nc.sbuf_tensor(name, shape, dt))
    Wq = SA_("Wqb", [128, 8, D], BF16); b_Wq = bf("Wqb")
    Wk = SA_("Wkb", [128, 8, D], BF16); b_Wk = bf("Wkb")
    Wv = SA_("Wvb", [128, 8, D], BF16); b_Wv = bf("Wvb")
    Wo = SA_("Wob", [128, 8, D], BF16); b_Wo = bf("Wob")
    Wsm = SA_("Wsmb", [128, 8, 12], BF16); b_Wsm = bf("Wsmb")

    esI = ExitStack()
    SI = lambda name, shape, dt: esI.enter_context(nc.sbuf_tensor(name, shape, dt))
    row = SI("row", [1, 16 * 1024], F32); b_row = bf("row")
    rowbf = SI("rowbf", [1, 1024], BF16); b_rowbf = bf("rowbf")
    c_cm = SI("c_cm", [128, 8], BF16); b_ccm = bf("c_cm")
    wld = [SI("wld0", [128, 2, 512], F32), SI("wld1", [128, 2, 512], F32)]
    b_wld = [bf("wld0"), bf("wld1")]
    wcs = SI("wcs", [128, 2, 512], BF16); b_wcs = bf("wcs")
    wsmf = SI("wsmf", [128, 8, 12], F32); b_wsmf = bf("wsmf")

    R_MOD, R_CSIL, R_A, R_B, R_C, R_Dr, R_GA, R_GB, R_WK = \
        0, 6144, 7168, 8192, 9216, 10240, 11264, 12288, 13312

    dmald(cfs[:, :], cf_h.ap(), writes=[b_cf])
    dmald(cbs[:, :], cb_h.ap(), writes=[b_cb])
    dmald(corrb[0:1, :], corrb_d.ap(), writes=[b_corrb])
    dmald(mixcol[0:4, 0:1], mixb_d.ap(), writes=[b_mixc])
    p.dma("ld", lambda E: E.dma_start(rms_col[:, 0:2], bass.AP(rmsw_d, 0, [[1, 128], [128, 2]]),
                                      allow_slow_non_contiguous=True), writes=[b_rms])
    for j, cvd in enumerate((convq_d, convk_d, convv_d)):
        dmald(cw[:, j * 32:(j + 1) * 32],
              bass.AP(cvd, 0, [[4, 128], [512, 8], [1, 4]]), writes=[b_cw])
    dmald(cwm[0:4, :], convm_d.ap(), writes=[b_cwm])
    dmald(row[0:1, R_CSIL + 1024:R_CSIL + 2048], cvec_d.ap(), writes=[b_row])
    dmald(wsmf[:, :, :], bass.AP(Wsm_d, 0, [[12, 128], [128 * 12, 8], [1, 12]]),
          writes=[b_wsmf])

    mm(PSA[:, 16:24], KF("onesf", np_=1), corrb[0:1, :], True, True,
       reads=[b_corrb, b_cf], writes=[b_ps[0]])
    act(corrb[:, :], PSA[:, 16:24], AF.Identity, reads=[b_ps[0]], writes=[b_corrb])

    act(row[0:1, R_CSIL:R_CSIL + 1024], row[0:1, R_CSIL + 1024:R_CSIL + 2048], AF.Silu,
        reads=[b_row], writes=[b_row])
    cpy("dve", rowbf[0:1, :], row[0:1, R_CSIL:R_CSIL + 1024], reads=[b_row], writes=[b_rowbf])
    for kc in range(8):
        mm(PSA[:, kc:kc + 1], rowbf[0:1, kc * 128:(kc + 1) * 128],
           KB("ones", np_=1, nc_=1), True, True, reads=[b_rowbf, b_cb], writes=[b_ps[0]])
    act(c_cm[:, :], PSA[:, 0:8], AF.Identity, reads=[b_ps[0]], writes=[b_ccm])

    for ns in range(12):
        for kg in range(4):
            w = wld[kg % 2]; bb = b_wld[kg % 2]
            off = (256 * kg) * (6 * D) + 512 * ns
            dmald(w[:, :, :], bass.AP(wmod_d, off, [[6 * D, 128], [128 * 6 * D, 2], [1, 512]]),
                  writes=[bb])
            cpy("dve", wcs[:, :, :], w[:, :, :], reads=[bb], writes=[b_wcs])
            for j in range(2):
                kc = kg * 2 + j
                mm(PSB[0:1, 0:512], c_cm[:, kc:kc + 1], wcs[:, j, :],
                   kc == 0, kc == 7, reads=[b_wcs, b_ccm], writes=[b_ps[1]])
        act(row[0:1, R_MOD + ns * 512:R_MOD + (ns + 1) * 512], PSB[0:1, 0:512], AF.Identity,
            reads=[b_ps[1]], writes=[b_row])

    for src, dst in ((2 * D, R_GA), (5 * D, R_GB)):
        g = row[0:1, R_MOD + src:R_MOD + src + D]
        ts(row[0:1, R_A:R_A + D], g, 0.25, None, OP.max, None, reads=[b_row], writes=[b_row])
        ts(row[0:1, R_B:R_B + D], g, -0.25, None, OP.min, None, reads=[b_row], writes=[b_row])
        ts(row[0:1, R_C:R_C + D], g, 0.0, None, OP.is_ge, None, reads=[b_row], writes=[b_row])
        tt(row[0:1, R_Dr:R_Dr + D], row[0:1, R_A:R_A + D], row[0:1, R_B:R_B + D],
           OP.subtract, reads=[b_row], writes=[b_row])
        tt(row[0:1, R_A:R_A + D], row[0:1, R_C:R_C + D], row[0:1, R_Dr:R_Dr + D],
           OP.mult, reads=[b_row], writes=[b_row])
        tt(row[0:1, dst:dst + D], row[0:1, R_A:R_A + D], row[0:1, R_B:R_B + D],
           OP.add, reads=[b_row], writes=[b_row])

    for roff, dst, b_dst in ((R_GA, ga_b, b_gab), (R_GB, gb_b, b_gbb)):
        for half in range(2):
            mm(PSB[:, 0:512], KF("onesf", np_=1), row[0:1, roff + half * 512:roff + (half + 1) * 512],
               True, True, reads=[b_row, b_cf], writes=[b_ps[1]])
            act(dst[:, half * 512:(half + 1) * 512], PSB[:, 0:512], AF.Identity,
                reads=[b_ps[1]], writes=[b_dst])

    dump("mod", row[0:1, 0:4096], [b_row], 0, 4096, 1)
    dump("mod2", row[0:1, 4096:6144], [b_row], 0, 2048, 1)
    dump("gab", ga_b[:, :], [b_gab], 0, 1024)
    dump("gbb", gb_b[:, :], [b_gbb], 0, 1024)

    for blk, (src, add1) in enumerate(((1 * D, True), (0 * D, False),
                                      (4 * D, True), (3 * D, False))):
        if add1:
            ts(row[0:1, R_WK:R_WK + D], row[0:1, R_MOD + src:R_MOD + src + D], 1.0, None,
               OP.add, None, reads=[b_row], writes=[b_row])
            srcap = row[0:1, R_WK:R_WK + D]
        else:
            srcap = row[0:1, R_MOD + src:R_MOD + src + D]
        cpy("dve", rowbf[0:1, :], srcap, reads=[b_row], writes=[b_rowbf])
        for kc in range(8):
            mm(PSA[:, kc:kc + 1], rowbf[0:1, kc * 128:(kc + 1) * 128],
               KB("ones", np_=1, nc_=1), True, True, reads=[b_rowbf, b_cb], writes=[b_ps[0]])
        act(mod_cols[:, blk * 8:(blk + 1) * 8], PSA[:, 0:8], AF.Identity,
            reads=[b_ps[0]], writes=[b_modc])

    dump("modc", mod_cols[:, :], [b_modc], 0, 32)
    for Wd, Wt, bw in ((Wq_d, Wq, b_Wq), (Wk_d, Wk, b_Wk), (Wv_d, Wv, b_Wv), (Wo_d, Wo, b_Wo)):
        for kc in range(8):
            w = wld[kc % 2]; bb = b_wld[kc % 2]
            dmald(w[:, :, :], bass.AP(Wd, kc * 128 * D, [[D, 128], [512, 2], [1, 512]]),
                  writes=[bb])
            cpy("dve", Wt[:, kc, 0:512], w[:, 0, :], reads=[bb], writes=[bw])
            cpy("dve", Wt[:, kc, 512:1024], w[:, 1, :], reads=[bb], writes=[bw])
    cpy("dve", Wsm[:, :, :], wsmf[:, :, :], reads=[b_wsmf], writes=[b_Wsm])

    esI.close()

    esA2 = ExitStack()
    SA2_ = lambda name, shape, dt: esA2.enter_context(nc.sbuf_tensor(name, shape, dt))
    xa = [SA2_("xa0", [128, 2, D], F32), SA2_("xa1", [128, 2, D], F32)]
    b_xa = [bf("xa0"), bf("xa1")]
    xntk2 = SA2_("xntk2", [128, 2, 1024], BF16)
    b_xntk2 = [bf("xntkA"), bf("xntkB")]
    stat = SA2_("stat", [128, 8], F32); b_stat = bf("stat")
    xn_cm = SA2_("xn_cm", [128, 8, 256], BF16); b_xn = bf("xn_cm")
    qpre = SA2_("qpre", [128, 8, 259], BF16); b_qpre = bf("qpre")
    kpre = SA2_("kpre", [128, 8, 259], BF16); b_kpre = bf("kpre")
    vpre = SA2_("vpre", [128, 8, 259], BF16); b_vpre = bf("vpre")
    mpre = SA2_("mpre", [4, 259], BF16); b_mpre = bf("mpre")
    cm4 = SA2_("cm4", [4, 256], F32); b_cm4 = bf("cm4")
    mix_s = SA2_("mix_s", [4, 256], F32); b_mixs = bf("mix_s")
    cacc = SA2_("cacc", [128, 8, 256], BF16); b_cacc = bf("cacc")
    q_cm2 = [SA2_("q_cmA", [128, 8, 256], BF16), SA2_("q_cmB", [128, 8, 256], BF16)]
    b_qcm2 = [bf("q_cmA"), bf("q_cmB")]
    k_cm = SA2_("k_cm", [128, 8, 256], BF16); b_kcm = bf("k_cm")
    v_cm = SA2_("v_cm", [128, 8, 256], BF16); b_vcm = bf("v_cm")
    k_n, b_kn = k_cm, b_kcm
    sg4 = SA2_("sg4", [4, 768], F32); b_sg = bf("sg4")
    sgf = SA2_("sgf", [4, 256], F32); b_sgf = bf("sgf")
    sgb = SA2_("sgb", [4, 1408], BF16); b_sgb = bf("sgb")
    b_sgbRR = bf("sgbRR")
    sgt = SA2_("sgt", [128, 2, 4], F32); b_sgt = bf("sgt")
    nbcol = SA2_("nbcol", [128, 2, 4], F32); b_nbcol = bf("nbcol")
    SGB_G, SGB_M, SGB_NB, SGB_RNQ, SGB_RNK, SGB_RR = 0, 256, 512, 768, 1024, 1280
    gamb = SA2_("gamb", [128, 4, 256], BF16); b_gamb = bf("gamb")
    mixbt2 = [SA2_("mixbtA", [128, 4, 256], BF16), SA2_("mixbtB", [128, 4, 256], BF16)]
    b_mixbt2 = [bf("mixbtA"), bf("mixbtB")]
    rnqb = SA2_("rnqb", [128, 4, 256], BF16); b_rnqb = bf("rnqb")
    rnkb = SA2_("rnkb", [128, 4, 256], BF16); b_rnkb = bf("rnkb")
    ema = [SA2_("emaA", [128, 8, 256], BF16), SA2_("emaB", [128, 8, 256], BF16)]
    b_ema = [bf("emaA"), bf("emaB")]
    kntok2 = [SA2_("kntokA", [128, 4, 256], BF16), SA2_("kntokB", [128, 4, 256], BF16)]
    b_kntok2 = [bf("kntokA"), bf("kntokB")]
    nkbtok2 = [SA2_("nkbtokA", [128, 4, 256], BF16), SA2_("nkbtokB", [128, 4, 256], BF16)]
    b_nkbtok2 = [bf("nkbtokA"), bf("nkbtokB")]
    vbtok2 = [SA2_("vbtokA", [128, 4, 256], BF16), SA2_("vbtokB", [128, 4, 256], BF16)]
    b_vbtok2 = [bf("vbtokA"), bf("vbtokB")]
    wnegb = SA2_("wnegb", [128, 8, 128], BF16)
    b_wnegb2 = [bf("wnegbA"), bf("wnegbB")]
    b_wneg_h = lambda c: b_wnegb2[0] if c < 4 else b_wnegb2[1]
    uadjb = SA2_("uadjb", [128, 4, 256], BF16); b_uadjb = bf("uadjb")
    b_bema = b_uadjb
    mscr = SA2_("mscr", [128, 512], BF16); b_mscr = bf("mscr")
    Apb = [SA2_("Ap0", [128, 512], BF16), SA2_("Ap1", [128, 512], BF16),
           SA2_("Ap2", [128, 512], BF16), SA2_("Ap3", [128, 512], BF16)]
    b_Ap = [bf("Ap0"), bf("Ap1"), bf("Ap2"), bf("Ap3")]
    Upb = [SA2_("Up0", [128, 512], BF16), SA2_("Up1", [128, 512], BF16),
           SA2_("Up2", [128, 512], BF16), SA2_("Up3", [128, 512], BF16)]
    b_Up = [bf("Up0"), bf("Up1"), bf("Up2"), bf("Up3")]
    Ptb = [SA2_("Pt0", [128, 512], BF16), SA2_("Pt1", [128, 512], BF16),
           SA2_("Pt2", [128, 512], BF16), SA2_("Pt3", [128, 512], BF16)]
    b_Pt = [bf("Pt0"), bf("Pt1"), bf("Pt2"), bf("Pt3")]

    S_bf = SA2_("S_bf", [128, 8, 256], BF16)
    b_Sbf2 = [bf("S_bfA"), bf("S_bfB")]
    b_Sbf_h = lambda c: b_Sbf2[0] if c < 4 else b_Sbf2[1]
    S_ps = PSS
    b_Sps = b_pss
    bl1 = SA2_("bl1", [128, 8, 128], BF16)
    bl2 = SA2_("bl2", [128, 8, 128], F32)
    b_bl1h = [bf("bl1A"), bf("bl1B")]
    b_bl2h = [bf("bl2A"), bf("bl2B")]
    atnT2 = [SA2_("atnT0", [128, 512], BF16), SA2_("atnT1", [128, 512], BF16)]
    b_atnT2 = [bf("atnT0"), bf("atnT1")]
    onb = wnegb
    rt = nc.virtual_sbuf_tensor("rt", [128, D], F32)
    nc.back_virtual_tensor_with(rt, bl2)


    memset("dve", ema[1][:, :, 255:256], 0.0, reads=[], writes=[b_ema[1]])
    memset("dve", S_bf[:, :, :], 0.0, reads=[], writes=[b_Sbf2[0], b_Sbf2[1]])
    memset("dve", qpre[:, :, 0:3], 0.0, reads=[], writes=[b_qpre])
    memset("dve", kpre[:, :, 0:3], 0.0, reads=[], writes=[b_kpre])
    memset("dve", vpre[:, :, 0:3], 0.0, reads=[], writes=[b_vpre])
    memset("dve", mpre[0:4, 0:3], 0.0, reads=[], writes=[b_mpre])

    def ln_stats(xbuf, b_xbuf, r):
        xk, b_xk = xntk2[:, r, :], b_xntk2[r]
        act(xk, xbuf[:, r, :], AF.Square, reads=[b_xbuf], writes=[b_xk])
        red(stat[:, 0:1], xbuf[:, r, :], OP.add, reads=[b_xbuf], writes=[b_stat])
        red(stat[:, 1:2], xk, OP.add, reads=[b_xk], writes=[b_stat])
        ts(stat[:, 2:3], stat[:, 0:1], 1.0 / D, None, OP.mult, None,
           reads=[b_stat], writes=[b_stat])
        ts(stat[:, 3:4], stat[:, 1:2], 1.0 / D, None, OP.mult, None,
           reads=[b_stat], writes=[b_stat])
        tt(stat[:, 4:5], stat[:, 2:3], stat[:, 2:3], OP.mult, reads=[b_stat], writes=[b_stat])
        tt(stat[:, 5:6], stat[:, 3:4], stat[:, 4:5], OP.subtract,
           reads=[b_stat], writes=[b_stat])
        act(stat[:, 6:7], stat[:, 5:6], AF.Ln, bias=1e-5, reads=[b_stat], writes=[b_stat])
        act(stat[:, 7:8], stat[:, 6:7], AF.Exp, scale=-0.5, reads=[b_stat], writes=[b_stat])
        stt(stat[:, 4:5], stat[:, 2:3], -1.0, stat[:, 7:8], OP.mult, OP.mult,
            reads=[b_stat], writes=[b_stat])
        act(xk, xbuf[:, r, :], AF.Identity, scale=stat[:, 7:8], bias=stat[:, 4:5],
            reads=[b_xbuf, b_stat], writes=[b_xk])

    def ln_trans(r, cols):
        xk, b_xk = xntk2[:, r, :], b_xntk2[r]
        for c in range(8):
            bank = b_ps[c // 4]
            pst = PSX[c // 4]
            mm(pst[:, (c % 4) * 128:(c % 4 + 1) * 128],
               xk[:, c * 128:(c + 1) * 128], IDB, True, True,
               reads=[b_xk, b_cb], writes=[bank])
        for c in range(8):
            pst = PSX[c // 4]
            act(xn_cm[:, c, r * 128:(r + 1) * 128], pst[:, (c % 4) * 128:(c % 4 + 1) * 128],
                AF.Identity, scale=mod_cols[:, cols[0] * 8 + c:cols[0] * 8 + c + 1],
                bias=mod_cols[:, cols[1] * 8 + c:cols[1] * 8 + c + 1],
                reads=[b_ps[c // 4], b_modc], writes=[b_xn])

    def conv_taps(pre, b_pre, cwoff, dstbuf, b_dst):
        for c in range(8):
            ts(dstbuf[:, c, :], pre[:, c, 0:256],
               cw[:, cwoff + c * 4:cwoff + c * 4 + 1], None, OP.mult, None,
               reads=[b_pre, b_cw], writes=[b_dst])
        for k in range(1, 4):
            for c in range(8):
                stt(dstbuf[:, c, :], pre[:, c, k:k + 256],
                    cw[:, cwoff + c * 4 + k:cwoff + c * 4 + k + 1], dstbuf[:, c, :],
                    OP.mult, OP.add, reads=[b_pre, b_cw, b_dst], writes=[b_dst])

    NCH = 2 * NT

    def a_front1_stats(i):
        t0 = i * TA
        xb, b_xb = xa[i % 2], b_xa[i % 2]
        for r in range(2):
            dmald(xb[:, r, :], x_in[t0 + 128 * r:t0 + 128 * r + 128, :], writes=[b_xb])
        for r in range(2):
            ln_stats(xb, b_xb, r)

    def a_front1(i):
        for r in range(2):
            ln_trans(r, (0, 1))
        for (Wt, bw, pre, b_pre, cwoff) in ((Wq, b_Wq, qpre, b_qpre, 0),
                                            (Wk, b_Wk, kpre, b_kpre, 32),
                                            (Wv, b_Wv, vpre, b_vpre, 64)):
            if i > 0:
                cpy("dve", pre[:, :, 0:3], pre[:, :, 256:259], reads=[b_pre], writes=[b_pre])
            for m in range(8):
                slot = m % 4
                ap_ = PSS[:, (slot // 2) * 512 + (slot % 2) * 256:
                          (slot // 2) * 512 + (slot % 2) * 256 + 256]
                bank = b_pssq[slot // 2]
                for kc in range(8):
                    mm(ap_, Wt[:, kc, m * 128:(m + 1) * 128], xn_cm[:, kc, :],
                       kc == 0, kc == 7, reads=[bw, b_xn], writes=[bank])
                act(pre[:, m, 3:259], ap_, AF.Identity, reads=[bank], writes=[b_pre])

        if i > 0:
            cpy("dve", mpre[0:4, 0:3], mpre[0:4, 256:259], reads=[b_mpre], writes=[b_mpre])
        SPR = PSS[0:4, 512:768]
        for g3 in range(3):
            for kc in range(8):
                mm(SPR, Wsm[:, kc, g3 * 4:g3 * 4 + 4], xn_cm[:, kc, :],
                   kc == 0, kc == 7, reads=[b_Wsm, b_xn], writes=[b_pssq[1]])
            if g3 == 0:
                act(sg4[0:4, 0:256], SPR, AF.Sigmoid, reads=[b_pssq[1]], writes=[b_sg])
            elif g3 == 1:
                act(sg4[0:4, 256:512], SPR, AF.Sigmoid, reads=[b_pssq[1]], writes=[b_sg])
            else:
                act(mpre[0:4, 3:259], SPR, AF.Identity, reads=[b_pssq[1]], writes=[b_mpre])


    def a_front2(i):
        t0 = i * TA
        xb, b_xb = xa[i % 2], b_xa[i % 2]
        conv_taps(qpre, b_qpre, 0, cacc, b_cacc)
        q_cm, b_qcm = q_cm2[i % 2], b_qcm2[i % 2]
        mixbt, b_mixbt = mixbt2[i % 2], b_mixbt2[i % 2]
        act(cacc[:, :, :], cacc[:, :, :], AF.Silu, reads=[b_cacc], writes=[b_cacc])
        act(q_cm[:, :, :], cacc[:, :, :], AF.Silu, reads=[b_cacc], writes=[b_qcm])
        conv_taps(kpre, b_kpre, 32, cacc, b_cacc)
        act(cacc[:, :, :], cacc[:, :, :], AF.Silu, reads=[b_cacc], writes=[b_cacc])
        act(k_cm[:, :, :], cacc[:, :, :], AF.Silu, reads=[b_cacc], writes=[b_kcm])
        conv_taps(vpre, b_vpre, 64, cacc, b_cacc)
        act(v_cm[:, :, :], cacc[:, :, :], AF.Silu, reads=[b_cacc], writes=[b_vcm])
        ts(cm4[0:4, :], mpre[0:4, 0:256], cwm[0:4, 0:1], None, OP.mult, None,
           reads=[b_mpre, b_cwm], writes=[b_cm4])
        stt(mix_s[0:4, :], mpre[0:4, 1:257], cwm[0:4, 1:2], cm4[0:4, :], OP.mult, OP.add,
            reads=[b_mpre, b_cwm, b_cm4], writes=[b_mixs])
        stt(cm4[0:4, :], mpre[0:4, 2:258], cwm[0:4, 2:3], mix_s[0:4, :], OP.mult, OP.add,
            reads=[b_mpre, b_cwm, b_mixs], writes=[b_cm4])
        stt(mix_s[0:4, :], mpre[0:4, 3:259], cwm[0:4, 3:4], cm4[0:4, :], OP.mult, OP.add,
            reads=[b_mpre, b_cwm, b_cm4], writes=[b_mixs])
        act(cm4[0:4, :], mix_s[0:4, :], AF.Silu, reads=[b_mixs], writes=[b_cm4])
        act(sg4[0:4, 512:768], cm4[0:4, :], AF.Sigmoid, bias=mixcol[0:4, 0:1],
            reads=[b_cm4, b_mixc], writes=[b_sg])

        cpy("dve", sgb[0:4, 0:512], sg4[0:4, 256:768], reads=[b_sg], writes=[b_sgb])
        ts(sgb[0:4, SGB_NB:SGB_NB + 256], sg4[0:4, 0:256], -1.0, None,
           OP.mult, None, reads=[b_sg], writes=[b_sgb])

    def a_front2b(i):
        t0 = i * TA
        q_cm, b_qcm = q_cm2[i % 2], b_qcm2[i % 2]
        mixbt, b_mixbt = mixbt2[i % 2], b_mixbt2[i % 2]
        for r in range(2):
            mm(PSS[:, r * 4:r * 4 + 4], sg4[0:4, r * 128:(r + 1) * 128],
               KF("ident", np_=4, nc_=4), True, True, reads=[b_sg, b_cf], writes=[b_pssq[0]])
        for r in range(2):
            act(sgt[:, r, :], PSS[:, r * 4:r * 4 + 4], AF.Identity,
                reads=[b_pssq[0]], writes=[b_sgt])
            ts(nbcol[:, r, :], sgt[:, r, 0:4], -1.0, None, OP.mult, None,
               reads=[b_sgt], writes=[b_nbcol])

        for dst, b_dst, seg in ((gamb, b_gamb, SGB_G), (mixbt, b_mixbt, SGB_M)):
            for h in range(4):
                sl_ = PSS[:, (h % 2) * 256:(h % 2) * 256 + 256]
                mm(sl_, KB("selrow", np_=4, c0=h * 128, nc_=128),
                   sgb[0:4, seg:seg + 256], True, True,
                   reads=[b_sgb, b_cb], writes=[b_pssq[0]])
                act(dst[:, h, :], sl_, AF.Identity, reads=[b_pssq[0]], writes=[b_dst])

        for h4 in range(2):
            u4 = bass.AP(uadjb, 0, [[1024, 128], [512, 2], [256, 2], [1, 256]])
            v4 = bass.AP(v_cm, h4 * 1024, [[2048, 128], [512, 2], [256, 2], [1, 256]])
            g4 = bass.AP(gamb, 2 * h4 * 256, [[1024, 128], [256, 2], [0, 2], [1, 256]])
            tt(u4, v4, g4, OP.mult, reads=[b_gamb, b_vcm], writes=[b_bema])
            tt(uadjb[:, :, :], v_cm[:, 4 * h4:4 * h4 + 4, :], uadjb[:, :, :],
               OP.subtract, reads=[b_vcm, b_bema], writes=[b_bema])
            for j in range(4):
                c = 4 * h4 + j
                scan(ema[i % 2][:, c, :], gamb[:, c // 2, :], uadjb[:, j, :],
                     ema[(i + 1) % 2][:, c, 255:256],
                     reads=[b_gamb, b_bema, b_ema[(i + 1) % 2]], writes=[b_ema[i % 2]])

        if i == 0:
            if "xnf" in dbg_d:
                tt(bl1[:, :, :], xn_cm[:, :, 0:128], xn_cm[:, :, 0:128], OP.max,
                   reads=[b_xn], writes=[b_bl1h[0], b_bl1h[1]])
                dump("xnf", bl1[:, :, :], [b_bl1h[0], b_bl1h[1]], 0, 1024)
            dump("xn", xn_cm[:, :, :], [b_xn], 0, 2048)
            dump("qcm", q_cm[:, :, :], [b_qcm], 0, 2048)
            dump("kcm", k_cm[:, :, :], [b_kcm], 0, 2048)
            dump("vcm", v_cm[:, :, :], [b_vcm], 0, 2048)
            dump("sg", sg4[0:4, 0:768], [b_sg], 0, 768, 4)
            dump("ema", ema[0][:, :, :], [b_ema[0]], 0, 2048)
        tt(cacc[:, :, :], q_cm[:, :, :], q_cm[:, :, :], OP.mult, reads=[b_qcm], writes=[b_cacc])
        for kc in range(8):
            mm(PSS[0:4, 512:768], KB("hind", c0=4 * (kc // 2), nc_=4), cacc[:, kc, :],
               kc == 0, kc == 7, reads=[b_cacc, b_cb], writes=[b_pssq[1]])
        act(sgf[0:4, :], PSS[0:4, 512:768], AF.Ln, bias=1e-6, reads=[b_pssq[1]], writes=[b_sgf])
        act(sgb[0:4, SGB_RNQ:SGB_RNQ + 256], sgf[0:4, :], AF.Exp, scale=-0.5,
            reads=[b_sgf], writes=[b_sgb])
        tt(cacc[:, :, :], k_cm[:, :, :], k_cm[:, :, :], OP.mult, reads=[b_kcm], writes=[b_cacc])
        for kc in range(8):
            mm(PSS[0:4, 512:768], KB("hind", c0=4 * (kc // 2), nc_=4), cacc[:, kc, :],
               kc == 0, kc == 7, reads=[b_cacc, b_cb], writes=[b_pssq[1]])
        act(sgf[0:4, :], PSS[0:4, 512:768], AF.Ln, bias=1e-6, reads=[b_pssq[1]], writes=[b_sgf])
        act(sgb[0:4, SGB_RNK:SGB_RNK + 256], sgf[0:4, :], AF.Exp, scale=-0.5,
            reads=[b_sgf], writes=[b_sgb])
        for dst, b_dst, seg in ((rnqb, b_rnqb, SGB_RNQ), (rnkb, b_rnkb, SGB_RNK)):
            for h in range(4):
                sl_ = PSS[:, (h % 2) * 256:(h % 2) * 256 + 256]
                mm(sl_, KB("selrow", np_=4, c0=h * 128, nc_=128), sgb[0:4, seg:seg + 256],
                   True, True, reads=[b_sgb, b_cb], writes=[b_pssq[0]])
                act(dst[:, h, :], sl_, AF.Identity, reads=[b_pssq[0]], writes=[b_dst])
        q4d = bass.AP(q_cm, 0, [[2048, 128], [512, 4], [256, 2], [1, 256]])
        k4d = bass.AP(k_cm, 0, [[2048, 128], [512, 4], [256, 2], [1, 256]])
        rq4 = bass.AP(rnqb, 0, [[1024, 128], [256, 4], [0, 2], [1, 256]])
        rk4 = bass.AP(rnkb, 0, [[1024, 128], [256, 4], [0, 2], [1, 256]])
        tt(q4d, q4d, rq4, OP.mult, reads=[b_qcm, b_rnqb], writes=[b_qcm])
        tt(k4d, k4d, rk4, OP.mult, reads=[b_kcm, b_rnkb], writes=[b_kcm])


    def a_delta(i):
        t0 = i * TA
        xb, b_xb = xa[i % 2], b_xa[i % 2]
        q_n, b_qn = q_cm2[i % 2], b_qcm2[i % 2]
        QB = lambda q: (PSS[:, q * 512:(q + 1) * 512], b_pssq[q])
        BANKS = [
            {"kt": [(PSA, b_ps[0]), (PSB, b_ps[1])], "vt": [(PSC, b_ps[2]), (PSD, b_ps[3])],
             "ga": (PSA, b_ps[0]), "gq": (PSB, b_ps[1]),
             "ub": [(PSC, b_ps[2]), (PSD, b_ps[3])],
             "a2": (PSC, b_ps[2]), "u2": (PSD, b_ps[3]), "pt": (PSB, b_ps[1])},
            {"kt": [QB(0), QB(1)], "vt": [QB(2), QB(3)],
             "ga": QB(0), "gq": QB(1), "ub": [QB(2), QB(3)],
             "a2": QB(2), "u2": QB(3), "pt": QB(1)},
        ]
        atnT_ap = lambda r, h: atnT2[r][:, h * 128:(h + 1) * 128]
        Tt_fin = [None, None]

        def d_trans(r):
            toff = r * 128
            bk = BANKS[r]
            for src, b_src, key in ((k_n, b_kn, "kt"), (v_cm, b_vcm, "vt")):
                for h in range(4):
                    for ci in range(2):
                        pst, bank = bk[key][h // 2]
                        mm(pst[:, ((h % 2) * 2 + ci) * 128:((h % 2) * 2 + ci) * 128 + 128],
                           src[:, 2 * h + ci, toff:toff + 128], IDB, True, True,
                           reads=[b_src, b_cb], writes=[bank])
            for hp in range(2):
                pst, bank = bk["kt"][hp]
                act(kntok2[r][:, 2 * hp:2 * hp + 2, :], pst[:, 0:512], AF.Identity,
                    reads=[bank], writes=[b_kntok2[r]])
            for h in range(4):
                pst, bank = bk["kt"][h // 2]
                sl = pst[:, (h % 2) * 256:(h % 2) * 256 + 256]
                act(nkbtok2[r][:, h, :], sl, AF.Identity, scale=nbcol[:, r, h:h + 1],
                    reads=[bank, b_nbcol], writes=[b_nkbtok2[r]])
            for h in range(4):
                pst, bank = bk["vt"][h // 2]
                sl = pst[:, (h % 2) * 256:(h % 2) * 256 + 256]
                act(vbtok2[r][:, h, :], sl, AF.Identity, scale=sgt[:, r, h:h + 1],
                    reads=[bank, b_sgt], writes=[b_vbtok2[r]])

        def d_gram(r):
            toff = r * 128
            bk = BANKS[r]
            ga, b_ga = bk["ga"]
            gq, b_gq = bk["gq"]
            for h in range(4):
                for ci in range(2):
                    mm(ga[:, h * 128:(h + 1) * 128], k_n[:, 2 * h + ci, toff:toff + 128],
                       k_n[:, 2 * h + ci, toff:toff + 128], ci == 0, ci == 1,
                       reads=[b_kn], writes=[b_ga])
            for h in range(4):
                for ci in range(2):
                    mm(gq[:, h * 128:(h + 1) * 128], k_n[:, 2 * h + ci, toff:toff + 128],
                       q_n[:, 2 * h + ci, toff:toff + 128], ci == 0, ci == 1,
                       reads=[b_kn, b_qn], writes=[b_gq])

        def d_mask(r):
            toff = r * 128
            bk = BANKS[r]
            ga, b_ga = bk["ga"]
            gq, b_gq = bk["gq"]
            for h in range(4):
                stt(Apb[2 * r][:, h * 128:(h + 1) * 128], ga[:, h * 128:(h + 1) * 128],
                    nbcol[:, r, h:h + 1], KF("strictlow"), OP.mult, OP.mult,
                    reads=[b_ga, b_nbcol, b_cf], writes=[b_Ap[2 * r]])
            for h in range(4):
                pst, bank = bk["ub"][h % 2]
                mm(pst[:, (h // 2) * 256:(h // 2) * 256 + 128],
                   KB("selrow", np_=4, c0=h * 128, nc_=128),
                   sgb[0:4, SGB_NB + toff:SGB_NB + toff + 128], True, True,
                   reads=[b_sgb, b_cb], writes=[bank])
            for h in range(4):
                pst, bank = bk["ub"][h % 2]
                tt(mscr[:, h * 128:(h + 1) * 128], pst[:, (h // 2) * 256:(h // 2) * 256 + 128],
                   KF("upstrict4", c0=h * 128, nc_=128), OP.mult,
                   reads=[bank, b_cf], writes=[b_mscr])
            tt(Upb[2 * r][:, :], ga[:, 0:512], mscr[:, :], OP.mult,
               reads=[b_ga, b_mscr], writes=[b_Up[2 * r]])
            tt(atnT2[r][:, :], gq[:, 0:512], KF("upincl4"), OP.mult,
               reads=[b_gq, b_cf], writes=[b_atnT2[r]])

        def d_tt(r, j, pa):
            bk = BANKS[r]
            na = 1 - pa
            a2, b_a2 = bk["a2"]
            u2, b_u2 = bk["u2"]
            ptp, b_ptp = bk["pt"]
            iA = lambda p: 2 * r + p
            for h in range(4):
                mm(a2[:, h * 128:(h + 1) * 128], Upb[iA(pa)][:, h * 128:(h + 1) * 128],
                   Apb[iA(pa)][:, h * 128:(h + 1) * 128], True, True,
                   reads=[b_Up[iA(pa)], b_Ap[iA(pa)]], writes=[b_a2])
            act(Apb[iA(na)][:, :], a2[:, 0:512], AF.Identity, reads=[b_a2],
                writes=[b_Ap[iA(na)]])
            if j < 6:
                for h in range(4):
                    mm(u2[:, h * 128:(h + 1) * 128], Apb[iA(pa)][:, h * 128:(h + 1) * 128],
                       Upb[iA(pa)][:, h * 128:(h + 1) * 128], True, True,
                       reads=[b_Up[iA(pa)], b_Ap[iA(pa)]], writes=[b_u2])
                act(Upb[iA(na)][:, :], u2[:, 0:512], AF.Identity, reads=[b_u2],
                    writes=[b_Up[iA(na)]])
            mm(ptp[:, 0:512], IDB, Ptb[iA(pa)][:, :], True, False,
               reads=[b_Pt[iA(pa)], b_cb], writes=[b_ptp])
            for h in range(4):
                mm(ptp[:, h * 128:(h + 1) * 128], Apb[iA(na)][:, h * 128:(h + 1) * 128],
                   Ptb[iA(pa)][:, h * 128:(h + 1) * 128], False, True,
                   reads=[b_Ap[iA(na)], b_Pt[iA(pa)]], writes=[b_ptp])
            act(Ptb[iA(na)][:, :], ptp[:, 0:512], AF.Identity, reads=[b_ptp],
                writes=[b_Pt[iA(na)]])
            return na

        for r in range(2):
            d_trans(r)
        for r in range(2):
            d_gram(r)
        for r in range(2):
            d_mask(r)
        for r in range(2):
            tt(Ptb[2 * r][:, :], Upb[2 * r][:, :], KB("id4"), OP.add,
               reads=[b_Up[2 * r], b_cb], writes=[b_Pt[2 * r]])
        pas = [0, 0]
        for j in range(1, 7):
            for r in range(2):
                pas[r] = d_tt(r, j, pas[r])
        for r in range(2):
            Tt_fin[r] = (Ptb[2 * r + pas[r]], b_Pt[2 * r + pas[r]])

        return Tt_fin

    def a_scans(i, Tt_fin):
        t0 = i * TA
        xb, b_xb = xa[i % 2], b_xa[i % 2]
        q_n, b_qn = q_cm2[i % 2], b_qcm2[i % 2]
        mixbt, b_mixbt = mixbt2[i % 2], b_mixbt2[i % 2]
        atnT_ap = lambda r, h: atnT2[r][:, h * 128:(h + 1) * 128]
        for r in range(2):
            cc = 2 * i + r
            toff = r * 128
            Tt, b_Tt = Tt_fin[r]
            kntok, b_kntok = kntok2[r], b_kntok2[r]
            nkbtok, b_nkbtok = nkbtok2[r], b_nkbtok2[r]
            vbtok, b_vbtok = vbtok2[r], b_vbtok2[r]
            for h in range(4):
                for ci in range(2):
                    pst, bank = PSX[h // 2], b_ps[h // 2]
                    mm(pst[:, ((h % 2) * 2 + ci) * 128:((h % 2) * 2 + ci) * 128 + 128],
                       nkbtok[:, h, ci * 128:ci * 128 + 128], Tt[:, h * 128:(h + 1) * 128],
                       True, True, reads=[b_nkbtok, b_Tt], writes=[bank])
            for hp in range(2):
                act(wnegb[:, 4 * hp:4 * hp + 4, :], PSX[hp][:, 0:512], AF.Identity,
                    reads=[b_ps[hp]], writes=[b_wnegb2[hp]])
            pump_dve(16)
            for h in range(4):
                pst, bank = PSX[2 + h // 2], b_ps[2 + h // 2]
                mm(pst[:, (h % 2) * 256:(h % 2) * 256 + 256], Tt[:, h * 128:(h + 1) * 128],
                   vbtok[:, h, :], True, False, reads=[b_Tt, b_vbtok], writes=[bank])
                for ci in range(2):
                    mm(pst[:, (h % 2) * 256:(h % 2) * 256 + 256], wnegb[:, 2 * h + ci, :],
                       S_bf[:, 2 * h + ci, :], False, ci == 1,
                       reads=[b_wneg_h(2 * h + ci), b_Sbf_h(2 * h + ci)], writes=[bank])
            for hp in range(2):
                act(uadjb[:, 2 * hp:2 * hp + 2, :], PSX[2 + hp][:, 0:512], AF.Identity,
                    reads=[b_ps[2 + hp]], writes=[b_uadjb])
            pump_dve(16)
            for h in range(4):
                pst, bank = PSX[h // 2], b_ps[h // 2]
                for m2 in range(2):
                    for ci in range(2):
                        mm(pst[:, ((h % 2) * 2 + m2) * 128:((h % 2) * 2 + m2) * 128 + 128],
                           S_bf[:, 2 * h + ci, m2 * 128:m2 * 128 + 128],
                           q_n[:, 2 * h + ci, toff:toff + 128], ci == 0, False,
                           reads=[b_Sbf_h(2 * h + ci), b_qn], writes=[bank])
                    mm(pst[:, ((h % 2) * 2 + m2) * 128:((h % 2) * 2 + m2) * 128 + 128],
                       uadjb[:, h, m2 * 128:m2 * 128 + 128], atnT_ap(r, h),
                       False, True, reads=[b_uadjb, b_atnT2[r]], writes=[bank])
            pump_dve(2)
            for wave in range(2):
                for k4 in range(4):
                    hc = wave * 4 + k4
                    h, ci = hc // 2, hc % 2
                    sl = S_ps[:, (2 + k4 // 2) * 512 + (k4 % 2) * 256:
                              (2 + k4 // 2) * 512 + (k4 % 2) * 256 + 256]
                    bq = b_pssq[2 + k4 // 2]
                    if cc > 0:
                        mm(sl, IDB, S_bf[:, hc, :], True, False,
                           reads=[b_cb, b_Sbf_h(hc)], writes=[bq])
                    mm(sl, kntok[:, h, ci * 128:ci * 128 + 128], uadjb[:, h, :],
                       cc == 0, True, reads=[b_kntok, b_uadjb], writes=[bq])
                for q2 in range(2):
                    act(S_bf[:, wave * 4 + 2 * q2:wave * 4 + 2 * q2 + 2, :],
                        S_ps[:, (2 + q2) * 512:(2 + q2) * 512 + 512],
                        AF.Identity, reads=[b_pssq[2 + q2]], writes=[b_Sbf2[wave]])
            for hb in range(2):
                tt(bl1[:, 4 * hb:4 * hb + 4, :],
                   ema[i % 2][:, 4 * hb:4 * hb + 4, toff:toff + 128], PSX[hb][:, 0:512],
                   OP.subtract, reads=[b_ema[i % 2], b_ps[hb]], writes=[b_bl1h[hb]])
                bl1h4 = bass.AP(bl1, hb * 512, [[1024, 128], [256, 2], [128, 2], [1, 128]])
                bl2h4 = bass.AP(bl2, hb * 512, [[1024, 128], [256, 2], [128, 2], [1, 128]])
                mx4 = bass.AP(mixbt, 2 * hb * 256 + toff,
                              [[1024, 128], [256, 2], [0, 2], [1, 128]])
                tt(bl2h4, bl1h4, mx4, OP.mult,
                   reads=[b_bl1h[hb], b_mixbt], writes=[b_bl2h[hb]])
                tt(bl1[:, 4 * hb:4 * hb + 4, :], PSX[hb][:, 0:512],
                   bl2[:, 4 * hb:4 * hb + 4, :], OP.add,
                   reads=[b_ps[hb], b_bl2h[hb]], writes=[b_bl1h[hb]])
                tt(wnegb[:, 4 * hb:4 * hb + 4, :], bl1[:, 4 * hb:4 * hb + 4, :],
                   bl1[:, 4 * hb:4 * hb + 4, :], OP.mult,
                   reads=[b_bl1h[hb]], writes=[b_wnegb2[hb]])
            for c in range(8):
                mm(PSC[0:4, 0:128], KB("hind", c0=4 * (c // 2), nc_=4),
                   wnegb[:, c, :], c == 0, c == 7,
                   reads=[b_wneg_h(c), b_cb], writes=[b_ps[2]])
            act(sgf[0:4, 0:128], PSC[0:4, 0:128], AF.Ln, scale=1.0 / 256, bias=1e-5,
                reads=[b_ps[2]], writes=[b_sgf])
            act(sgb[0:4, SGB_RR:SGB_RR + 128], sgf[0:4, 0:128], AF.Exp, scale=-0.5,
                reads=[b_sgf], writes=[b_sgbRR])
            for h in range(4):
                mm(PSD[:, h * 128:(h + 1) * 128], KB("selrow", np_=4, c0=h * 128, nc_=128),
                   sgb[0:4, SGB_RR:SGB_RR + 128], True, True, reads=[b_sgbRR, b_cb],
                   writes=[b_ps[3]])
            for c in range(8):
                stt(onb[:, c, :], bl1[:, c, :], rms_col[:, c % 2:c % 2 + 1],
                    PSD[:, (c // 2) * 128:(c // 2) * 128 + 128], OP.mult, OP.mult,
                    reads=[b_bl1h[c // 4], b_rms, b_ps[3]], writes=[b_wneg_h(c)])
            pump(14)
            for kc in range(8):
                mm(PSA[:, 0:512], onb[:, kc, :], Wo[:, kc, 0:512], kc == 0, kc == 7,
                   reads=[b_wneg_h(kc), b_Wo], writes=[b_ps[0]])
            for kc in range(8):
                mm(PSB[:, 0:512], onb[:, kc, :], Wo[:, kc, 512:1024], kc == 0, kc == 7,
                   reads=[b_wneg_h(kc), b_Wo], writes=[b_ps[1]])
            tt(rt[:, 0:512], PSA[:, 0:512], ga_b[:, 0:512], OP.mult,
               reads=[b_ps[0], b_gab], writes=[b_bl2h[0]])
            tt(rt[:, 512:1024], PSB[:, 0:512], ga_b[:, 512:1024], OP.mult,
               reads=[b_ps[1], b_gab], writes=[b_bl2h[1]])
            tt(xb[:, r, :], xb[:, r, :], rt[:, :], OP.add,
               reads=[b_xb, b_bl2h[0], b_bl2h[1]], writes=[b_xb])
            dmast(x1_d[t0 + r * 128:t0 + r * 128 + 128, :], xb[:, r, :],
                  reads=[b_xb])
            if "B2" not in phases:
                dmast(out_d[t0 + r * 128:t0 + r * 128 + 128, :], xb[:, r, :],
                      reads=[b_xb])
            if i == 0:
                dump("x1", xb[:, r, :], [b_xb], r * 1024, 1024)
            pump(8)
        pump(len(p.bgq))


    a_front1_stats(0)
    a_front1(0)
    a_front2(0)
    for i in range(NT):
        a_front2b(i)
        if i + 1 < NT:
            a_front1_stats(i + 1)
        tf = a_delta(i)
        if i + 1 < NT:
            a_front1(i + 1)
            defer_into_bgq(a_front2, i + 1)
        a_scans(i, tf)

    p.barrier()
    esA2.close()
    esA.close()

    if "B1" not in phases:
        with nc.Block() as block:
            p.replay(block)
        esg.close()
        return nc, p

    esB = ExitStack()
    SBb = lambda name, shape, dt: esB.enter_context(nc.sbuf_tensor(name, shape, dt))
    upT = SBb("upT", [128, 8, SHARED], BF16); b_upT = bf("upT")
    dnT = SBb("dnT", [128, 32, D], BF16); b_dnT = bf("dnT")
    ldb = [SBb("ldb0", [128, 1024], F32), SBb("ldb1", [128, 1024], F32)]
    b_ldb = [bf("ldb0"), bf("ldb1")]
    cstb = SBb("cstb", [128, 1024], BF16); b_cstb = bf("cstb")
    x1a = [SBb("x1a0", [128, 2, D], F32), SBb("x1a1", [128, 2, D], F32)]
    b_x1a = [bf("x1a0"), bf("x1a1")]
    xnt2 = SBb("xnt2", [128, 2, 1024], BF16)
    b_xnt2h = [bf("xnt2A"), bf("xnt2B")]
    stat2 = SBb("stat2", [128, 8], F32); b_stat2 = bf("stat2")
    h2c2 = [SBb("h2cA", [128, 8, 256], BF16), SBb("h2cB", [128, 8, 256], BF16)]
    b_h2c2 = [bf("h2cA"), bf("h2cB")]
    r1b = SBb("r1b", [128, 256], BF16); b_r1b = bf("r1b")
    rsh = SBb("rsh", [128, 32, 256], BF16); b_rsh = bf("rsh")
    rt2 = ldb[0]
    b_rt2 = b_ldb[0]
    accs1t = SBb("accs0", [128, D], F32)
    accs = [accs1t, accs1t]
    b_accs = [bf("accs0"), bf("accs0")]

    for rc in range(32):
        w, bb = ldb[rc % 2], b_ldb[rc % 2]
        dmald(w[:, :], up_d[rc * 128:(rc + 1) * 128, :], writes=[bb])
        cpy("dve", cstb[:, :], w[:, :], reads=[bb], writes=[b_cstb])
        for c in range(8):
            pst, bank = PSX[c // 4], b_ps[c // 4]
            mm(pst[:, (c % 4) * 128:(c % 4) * 128 + 128], cstb[:, c * 128:(c + 1) * 128],
               IDB, True, True, reads=[b_cstb, b_cb], writes=[bank])
        for c in range(8):
            pst = PSX[c // 4]
            act(upT[:, c, rc * 128:(rc + 1) * 128], pst[:, (c % 4) * 128:(c % 4) * 128 + 128],
                AF.Identity, reads=[b_ps[c // 4]], writes=[b_upT])
    for rc in range(8):
        for q4 in range(4):
            w, bb = ldb[q4 % 2], b_ldb[q4 % 2]
            dmald(w[:, :], bass.AP(down_d, rc * 128 * SHARED + q4 * 1024,
                                   [[SHARED, 128], [1, 1024]]), writes=[bb])
            cpy("dve", cstb[:, :], w[:, :], reads=[bb], writes=[b_cstb])
            for c in range(8):
                pst, bank = PSX[c // 4], b_ps[c // 4]
                mm(pst[:, (c % 4) * 128:(c % 4) * 128 + 128], cstb[:, c * 128:(c + 1) * 128],
                   IDB, True, True, reads=[b_cstb, b_cb], writes=[bank])
            for c in range(8):
                pst = PSX[c // 4]
                act(dnT[:, q4 * 8 + c, rc * 128:(rc + 1) * 128],
                    pst[:, (c % 4) * 128:(c % 4) * 128 + 128], AF.Identity,
                    reads=[b_ps[c // 4]], writes=[b_dnT])

    def ln2_stats(xbuf, b_xbuf, r):
        xk, b_xk = xnt2[:, r, :], b_xnt2h[r]
        act(xk, xbuf[:, r, :], AF.Square, reads=[b_xbuf], writes=[b_xk])
        red(stat2[:, 0:1], xbuf[:, r, :], OP.add, reads=[b_xbuf], writes=[b_stat2])
        red(stat2[:, 1:2], xk, OP.add, reads=[b_xk], writes=[b_stat2])
        ts(stat2[:, 2:3], stat2[:, 0:1], 1.0 / D, None, OP.mult, None,
           reads=[b_stat2], writes=[b_stat2])
        ts(stat2[:, 3:4], stat2[:, 1:2], 1.0 / D, None, OP.mult, None,
           reads=[b_stat2], writes=[b_stat2])
        tt(stat2[:, 4:5], stat2[:, 2:3], stat2[:, 2:3], OP.mult,
           reads=[b_stat2], writes=[b_stat2])
        tt(stat2[:, 5:6], stat2[:, 3:4], stat2[:, 4:5], OP.subtract,
           reads=[b_stat2], writes=[b_stat2])
        act(stat2[:, 6:7], stat2[:, 5:6], AF.Ln, bias=1e-5, reads=[b_stat2], writes=[b_stat2])
        act(stat2[:, 7:8], stat2[:, 6:7], AF.Exp, scale=-0.5, reads=[b_stat2], writes=[b_stat2])
        stt(stat2[:, 4:5], stat2[:, 2:3], -1.0, stat2[:, 7:8], OP.mult, OP.mult,
            reads=[b_stat2], writes=[b_stat2])
        act(xk, xbuf[:, r, :], AF.Identity, scale=stat2[:, 7:8], bias=stat2[:, 4:5],
            reads=[b_xbuf, b_stat2], writes=[b_xk])

    def ln2_trans(r, hc, b_hc):
        xk, b_xk = xnt2[:, r, :], b_xnt2h[r]
        for c in range(8):
            pst, bank = PSX[c // 4], b_ps[c // 4]
            mm(pst[:, (c % 4) * 128:(c % 4 + 1) * 128],
               xk[:, c * 128:(c + 1) * 128], IDB, True, True,
               reads=[b_xk, b_cb], writes=[bank])
        for c in range(8):
            pst = PSX[c // 4]
            act(hc[:, c, r * 128:(r + 1) * 128], pst[:, (c % 4) * 128:(c % 4 + 1) * 128],
                AF.Identity, scale=mod_cols[:, 16 + c:16 + c + 1],
                bias=mod_cols[:, 24 + c:24 + c + 1],
                reads=[b_ps[c // 4], b_modc], writes=[b_hc])

    def b1_front(i):
        t0 = i * TA
        xb, b_xb = x1a[i % 2], b_x1a[i % 2]
        for r in range(2):
            dmald(xb[:, r, :], x1_d[t0 + 128 * r:t0 + 128 * r + 128, :], writes=[b_xb])
        for r in range(2):
            ln2_stats(xb, b_xb, r)

    def b1_trans(i):
        hc, b_hc = h2c2[i % 2], b_h2c2[i % 2]
        for r in range(2):
            ln2_trans(r, hc, b_hc)

    b1_front(0)
    b1_trans(0)
    for i in range(NT):
        t0 = i * TA
        xb, b_xb = x1a[i % 2], b_x1a[i % 2]
        hc, b_hc = h2c2[i % 2], b_h2c2[i % 2]
        dmast(bass.AP(h2_d, t0, [[L, 128], [128 * L, 8], [1, 256]]), hc[:, :, :],
              reads=[b_hc])
        if i == 0:
            dump("h2", hc[:, :, :], [b_hc], 0, 2048)
        for ic in range(32):
            slot = ic % 4
            pst, bank = PSX[slot], b_ps[slot]
            for kc in range(8):
                mm(pst[:, 0:256], upT[:, kc, ic * 128:(ic + 1) * 128], hc[:, kc, :],
                   kc == 0, kc == 7, reads=[b_upT, b_hc], writes=[bank])
            act(r1b[:, :], pst[:, 0:256], AF.Relu, reads=[bank], writes=[b_r1b])
            tt(rsh[:, ic, :], r1b[:, :], r1b[:, :], OP.mult,
               reads=[b_r1b], writes=[b_rsh])
        if i + 1 < NT:
            b1_front(i + 1)
        for t2 in range(2):
            for dh in range(2):
                for ic in range(32):
                    mm(PSS[:, t2 * 1024 + dh * 512:t2 * 1024 + dh * 512 + 512],
                       rsh[:, ic, t2 * 128:t2 * 128 + 128], dnT[:, ic, dh * 512:dh * 512 + 512],
                       ic == 0, ic == 31, reads=[b_rsh, b_dnT],
                       writes=[b_pssq[t2 * 2 + dh]])
        if i + 1 < NT:
            b1_trans(i + 1)
        for r in range(2):
            tt(rt2[:, 0:512], PSS[:, r * 1024:r * 1024 + 512], gb_b[:, 0:512], OP.mult,
               reads=[b_pssq[2 * r], b_gbb], writes=[b_rt2])
            tt(rt2[:, 512:1024], PSS[:, r * 1024 + 512:r * 1024 + 1024], gb_b[:, 512:1024],
               OP.mult, reads=[b_pssq[2 * r + 1], b_gbb], writes=[b_rt2])
            tt(accs[r][:, :], xb[:, r, :], rt2[:, :], OP.add,
               reads=[b_xb, b_rt2], writes=[b_accs[r]])
            dmast(acc_d[t0 + r * 128:t0 + r * 128 + 128, :], accs[r][:, :],
                  reads=[b_accs[r]])
            if i == 0:
                dump("acc", accs[r][:, :], [b_accs[r]], r * 1024, 1024)

    p.barrier()
    esB.close()

    if "B2" not in phases:
        with nc.Block() as block:
            p.replay(block)
        esg.close()
        return nc, p

    esC = ExitStack()
    SC = lambda name, shape, dt: esC.enter_context(nc.sbuf_tensor(name, shape, dt))
    gT = SC("gT", [128, 8, 8], BF16); b_gT = bf("gT")
    f1T = SC("f1T", [128, 8, 512], BF16); b_f1T = bf("f1T")
    f2T = SC("f2T", [128, 4, 1024], BF16); b_f2T = bf("f2T")
    w1T = SC("w1T", [128, 32, 512], BF16); b_w1T = bf("w1T")
    w2T = SC("w2T", [128, 32, 512], BF16); b_w2T = bf("w2T")
    ldc = [SC("ldc0", [128, 1024], F32), SC("ldc1", [128, 1024], F32)]
    b_ldc = [bf("ldc0"), bf("ldc1")]
    cstc = SC("cstc", [128, 1024], BF16); b_cstc = bf("cstc")
    g8 = SC("g8", [8, 1024], BF16); b_g8 = bf("g8")
    h2t = [SC("h2t0", [128, 8, 512], BF16), SC("h2t1", [128, 8, 512], BF16)]
    b_h2t = [bf("h2t0"), bf("h2t1")]
    acct = [SC("acct0", [128, 4, D], F32), SC("acct1", [128, 4, D], F32)]
    b_acct = [bf("acct0"), bf("acct1")]
    sS = SC("sS", [128, 4, 8], F32); b_sS = bf("sS")
    we = SC("we", [128, 4, 8], F32); b_we = bf("we")
    rs4 = SC("rs4", [128, 4, 32], F32); b_rs = bf("rs4")
    latb = SC("latb", [128, 4, 512], BF16)
    b_latb4 = [bf("latb0"), bf("latb1"), bf("latb2"), bf("latb3")]
    r1q = SC("r1q", [128, 4, 512], BF16); b_r1q = bf("r1q")
    r2q = SC("r2q", [128, 4, 512], BF16); b_r2q = bf("r2q")
    combq = [SC("comb0", [128, 4, 512], F32), SC("comb1", [128, 4, 512], F32)]
    b_combq = [bf("comb0"), bf("comb1")]
    combbf = SC("combbf", [128, 4, 512], BF16); b_combbf = bf("combbf")
    combcm = SC("combcm", [128, 4, 512], BF16); b_combcm = bf("combcm")
    rt3 = SC("rt3", [128, D], F32); b_rt3 = bf("rt3")
    outs = [SC("outs0", [128, D], F32), SC("outs1", [128, D], F32)]
    b_outs = [bf("outs0"), bf("outs1")]

    dmald(ldc[0][0:8, :], gate_d.ap(), writes=[b_ldc[0]])
    cpy("dve", g8[0:8, :], ldc[0][0:8, :], reads=[b_ldc[0]], writes=[b_g8])
    for c in range(8):
        mm(PSA[:, (c % 4) * 128:(c % 4) * 128 + 8],
           g8[0:8, c * 128:(c + 1) * 128], KB("ident", np_=8, nc_=8), True, True,
           reads=[b_g8, b_cb], writes=[b_ps[0]])
        act(gT[:, c, :], PSA[:, (c % 4) * 128:(c % 4) * 128 + 8], AF.Identity,
            reads=[b_ps[0]], writes=[b_gT])
    for rc in range(4):
        w, bb = ldc[rc % 2], b_ldc[rc % 2]
        dmald(w[:, :], fc1_d[rc * 128:(rc + 1) * 128, :], writes=[bb])
        cpy("dve", cstc[:, :], w[:, :], reads=[bb], writes=[b_cstc])
        for c in range(8):
            pst, bank = PSX[c // 4], b_ps[c // 4]
            mm(pst[:, (c % 4) * 128:(c % 4) * 128 + 128], cstc[:, c * 128:(c + 1) * 128],
               IDB, True, True, reads=[b_cstc, b_cb], writes=[bank])
        for c in range(8):
            act(f1T[:, c, rc * 128:(rc + 1) * 128],
                PSX[c // 4][:, (c % 4) * 128:(c % 4) * 128 + 128], AF.Identity,
                reads=[b_ps[c // 4]], writes=[b_f1T])
    for rc in range(8):
        w, bb = ldc[rc % 2], b_ldc[rc % 2]
        dmald(w[:, 0:512], fc2_d[rc * 128:(rc + 1) * 128, :], writes=[bb])
        cpy("dve", cstc[:, 0:512], w[:, 0:512], reads=[bb], writes=[b_cstc])
        for c in range(4):
            pst, bank = PSX[c], b_ps[c]
            mm(pst[:, 0:128], cstc[:, c * 128:(c + 1) * 128], IDB, True, True,
               reads=[b_cstc, b_cb], writes=[bank])
        for c in range(4):
            act(f2T[:, c, rc * 128:(rc + 1) * 128], PSX[c][:, 0:128], AF.Identity,
                reads=[b_ps[c]], writes=[b_f2T])
    for wd, wt, bwt in ((w1_d, w1T, b_w1T), (w2_d, w2T, b_w2T)):
        for e in range(NE):
            for rc in range(4):
                w, bb = ldc[rc % 2], b_ldc[rc % 2]
                dmald(w[:, 0:512], bass.AP(wd, e * INTER * LAT + rc * 128 * 512,
                                           [[512, 128], [1, 512]]), writes=[bb])
                cpy("dve", cstc[:, 0:512], w[:, 0:512], reads=[bb], writes=[b_cstc])
                for c in range(4):
                    pst, bank = PSX[c], b_ps[c]
                    mm(pst[:, 0:128], cstc[:, c * 128:(c + 1) * 128], IDB, True, True,
                       reads=[b_cstc, b_cb], writes=[bank])
                for c in range(4):
                    act(wt[:, e * 4 + c, rc * 128:(rc + 1) * 128], PSX[c][:, 0:128], AF.Identity,
                        reads=[b_ps[c]], writes=[bwt])

    NT2 = NT // 2

    def b2_loads(i):
        t0 = i * 512
        dmald(h2t[i % 2][:, :, :], bass.AP(h2_d, t0, [[L, 128], [128 * L, 8], [1, 512]]),
              writes=[b_h2t[i % 2]])
        dmald(acct[i % 2][:, :, :], bass.AP(acc_d, t0 * D, [[D, 128], [128 * D, 4], [1, 1024]]),
              writes=[b_acct[i % 2]])

    b2_loads(0)
    for i in range(NT2):
        t0 = i * 512
        ht, b_ht = h2t[i % 2], b_h2t[i % 2]
        at, b_at = acct[i % 2], b_acct[i % 2]
        if i + 1 < NT2:
            b2_loads(i + 1)
        for tc in range(4):
            for kc in range(8):
                mm(PSA[0:128, tc * 128:tc * 128 + 8], ht[:, kc, tc * 128:(tc + 1) * 128],
                   gT[:, kc, :], kc == 0, kc == 7, reads=[b_ht, b_gT], writes=[b_ps[0]])
        act(sS[:, :, :], bass.AP(PSA, 0, [[512, 128], [128, 4], [1, 8]]), AF.Sigmoid,
            reads=[b_ps[0]], writes=[b_sS])
        R = lambda a, b2: rs4[:, :, a:b2]
        Rb = lambda a, w: bass.AP(rs4, a, [[128, 128], [32, 4], [0, w]])
        KFb = lambda name, w: bass.AP(cfs, CF[name][0], [[cf_np.shape[1], 128], [0, 4], [1, w]])
        CRb = bass.AP(corrb, 0, [[NE, 128], [0, 4], [1, NE]])
        tt(R(0, 8), sS[:, :, :], CRb, OP.add, reads=[b_sS, b_corrb], writes=[b_rs])
        if i == 0:
            dump("ra", rs4[:, 0, 0:24], [b_rs], 0, 24)
        for g in range(2):
            sl = R(4 * g, 4 * g + 4)
            red(R(16 + g, 17 + g), sl, OP.max, reads=[b_rs], writes=[b_rs])
            tt(R(8, 12), sl, Rb(16 + g, 4), OP.is_equal, reads=[b_rs], writes=[b_rs])
            tt(R(12, 16), R(8, 12), KFb("iota_mB", 4), OP.mult,
               reads=[b_rs, b_cf], writes=[b_rs])
            ts(R(12, 16), R(12, 16), 1000.0, None, OP.add, None, reads=[b_rs], writes=[b_rs])
            red(R(18, 19), R(12, 16), OP.min, reads=[b_rs], writes=[b_rs])
            tt(R(8, 12), KFb("iota8", 4), Rb(18, 4), OP.is_equal,
               reads=[b_cf, b_rs], writes=[b_rs])
            stt(R(12, 16), R(8, 12), -1e9, sl, OP.mult, OP.add, reads=[b_rs], writes=[b_rs])
            red(R(19, 20), R(12, 16), OP.max, reads=[b_rs], writes=[b_rs])
            tt(R(16 + g, 17 + g), R(16 + g, 17 + g), R(19, 20), OP.add,
               reads=[b_rs], writes=[b_rs])
        tt(R(20, 21), R(17, 18), R(16, 17), OP.is_gt, reads=[b_rs], writes=[b_rs])
        if i == 0:
            dump("rb", rs4[:, 0, 0:24], [b_rs], 0, 24)
        tt(R(8, 16), KFb("gdiff", 8), Rb(20, 8), OP.mult, reads=[b_cf, b_rs], writes=[b_rs])
        tt(R(8, 16), R(8, 16), KFb("glo", 8), OP.add, reads=[b_rs, b_cf], writes=[b_rs])
        tt(R(8, 16), R(0, 8), R(8, 16), OP.mult, reads=[b_rs], writes=[b_rs])
        if i == 0:
            dump("rc", rs4[:, 0, 0:24], [b_rs], 0, 24)
        red(R(16, 17), R(8, 16), OP.max, reads=[b_rs], writes=[b_rs])
        tt(R(0, 8), R(8, 16), Rb(16, 8), OP.is_equal, reads=[b_rs], writes=[b_rs])
        tt(R(0, 8), R(0, 8), KFb("iota_mB", 8), OP.mult, reads=[b_rs, b_cf], writes=[b_rs])
        ts(R(0, 8), R(0, 8), 1000.0, None, OP.add, None, reads=[b_rs], writes=[b_rs])
        red(R(17, 18), R(0, 8), OP.min, reads=[b_rs], writes=[b_rs])
        tt(R(0, 8), KFb("iota8", 8), Rb(17, 8), OP.is_equal,
           reads=[b_cf, b_rs], writes=[b_rs])
        if i == 0:
            dump("rd", rs4[:, 0, 0:24], [b_rs], 0, 24)
        stt(R(8, 16), R(0, 8), -1e9, R(8, 16), OP.mult, OP.add,
            reads=[b_rs], writes=[b_rs])
        if i == 0:
            dump("re", rs4[:, 0, 0:24], [b_rs], 0, 24)
        red(R(18, 19), R(8, 16), OP.max, reads=[b_rs], writes=[b_rs])
        tt(R(8, 16), R(8, 16), Rb(18, 8), OP.is_equal, reads=[b_rs], writes=[b_rs])
        if i == 0:
            dump("rf", rs4[:, 0, 0:24], [b_rs], 0, 24)
        tt(R(8, 16), R(8, 16), KFb("iota_mB", 8), OP.mult, reads=[b_rs, b_cf], writes=[b_rs])
        ts(R(8, 16), R(8, 16), 1000.0, None, OP.add, None, reads=[b_rs], writes=[b_rs])
        red(R(19, 20), R(8, 16), OP.min, reads=[b_rs], writes=[b_rs])
        tt(R(8, 16), KFb("iota8", 8), Rb(19, 8), OP.is_equal,
           reads=[b_cf, b_rs], writes=[b_rs])
        if i == 0:
            dump("rg", rs4[:, 0, 0:24], [b_rs], 0, 24)
        tt(R(24, 32), R(0, 8), sS[:, :, :], OP.mult, reads=[b_rs, b_sS], writes=[b_rs])
        red(R(20, 21), R(24, 32), OP.add, reads=[b_rs], writes=[b_rs])
        tt(R(24, 32), R(8, 16), sS[:, :, :], OP.mult, reads=[b_rs, b_sS], writes=[b_rs])
        red(R(21, 22), R(24, 32), OP.add, reads=[b_rs], writes=[b_rs])
        stt(R(22, 23), R(20, 21), 1e-20, R(21, 22), OP.add, OP.add,
            reads=[b_rs], writes=[b_rs])
        p.emit("dve", lambda E, o=R(23, 24), i_=R(22, 23): E.reciprocal(o, i_),
               reads=[b_rs], writes=[b_rs])
        tt(R(20, 21), R(20, 21), R(23, 24), OP.mult, reads=[b_rs], writes=[b_rs])
        tt(R(21, 22), R(21, 22), R(23, 24), OP.mult, reads=[b_rs], writes=[b_rs])
        tt(R(0, 8), R(0, 8), Rb(20, 8), OP.mult, reads=[b_rs], writes=[b_rs])
        tt(R(24, 32), R(8, 16), Rb(21, 8), OP.mult, reads=[b_rs], writes=[b_rs])
        tt(we[:, :, :], R(24, 32), R(0, 8), OP.add, reads=[b_rs], writes=[b_we])
        if i == 0:
            dump("we", we[:, :, :], [b_we], 0, 32)
        for lc in range(4):
            for kc in range(8):
                mm(PSS[:, lc * 512:(lc + 1) * 512], f1T[:, kc, lc * 128:(lc + 1) * 128],
                   ht[:, kc, :], kc == 0, kc == 7, reads=[b_f1T, b_ht],
                   writes=[b_pssq[lc]])
        for lc in range(4):
            act(latb[:, lc, :], PSS[:, lc * 512:(lc + 1) * 512], AF.Identity,
                reads=[b_pssq[lc]], writes=[b_latb4[lc]])
        def w1mms(e):
            for ic in range(4):
                for lc in range(4):
                    mm(PSS[:, ic * 512:(ic + 1) * 512], w1T[:, e * 4 + lc, ic * 128:(ic + 1) * 128],
                       latb[:, lc, :], lc == 0, lc == 3, reads=[b_w1T, b_latb4[lc]],
                       writes=[b_pssq[ic]])
        w1mms(0)
        for e in range(NE):
            for ic in range(4):
                act(r1q[:, ic, :], PSS[:, ic * 512:(ic + 1) * 512], AF.Relu,
                    reads=[b_pssq[ic]], writes=[b_r1q])
                tt(r2q[:, ic, :], r1q[:, ic, :], r1q[:, ic, :], OP.mult,
                   reads=[b_r1q], writes=[b_r2q])
            if e + 1 < NE:
                w1mms(e + 1)
            for tc in range(4):
                for ic in range(4):
                    mm(PSX[tc][:, 0:512], r2q[:, ic, tc * 128:(tc + 1) * 128],
                       w2T[:, e * 4 + ic, :], ic == 0, ic == 3,
                       reads=[b_r2q, b_w2T], writes=[b_ps[tc]])
            for tc in range(4):
                if e == 0:
                    ts(combq[0][:, tc, :], PSX[tc][:, 0:512], we[:, tc, 0:1], None,
                       OP.mult, None, reads=[b_ps[tc], b_we], writes=[b_combq[0]])
                else:
                    stt(combq[e % 2][:, tc, :], PSX[tc][:, 0:512], we[:, tc, e:e + 1],
                        combq[(e + 1) % 2][:, tc, :], OP.mult, OP.add,
                        reads=[b_ps[tc], b_we, b_combq[(e + 1) % 2]], writes=[b_combq[e % 2]])
        fin = (NE - 1) % 2
        for tc in range(4):
            act(combbf[:, tc, :], combq[fin][:, tc, :], AF.Identity,
                reads=[b_combq[fin]], writes=[b_combbf])
        for tc in range(4):
            for lc in range(4):
                pst, bank = PSX[lc], b_ps[lc]
                mm(pst[:, tc * 128:(tc + 1) * 128], combbf[:, tc, lc * 128:(lc + 1) * 128],
                   IDB, True, True, reads=[b_combbf, b_cb], writes=[bank])
        for lc in range(4):
            act(combcm[:, lc, :], PSX[lc][:, 0:512], AF.Identity,
                reads=[b_ps[lc]], writes=[b_combcm])
        for tc in range(4):
            pa_, pb_ = PSX[2 * (tc % 2)], PSX[2 * (tc % 2) + 1]
            ba_, bb_ = b_ps[2 * (tc % 2)], b_ps[2 * (tc % 2) + 1]
            for lc in range(4):
                mm(pa_[:, 0:512], combcm[:, lc, tc * 128:(tc + 1) * 128],
                   f2T[:, lc, 0:512], lc == 0, lc == 3, reads=[b_combcm, b_f2T], writes=[ba_])
            for lc in range(4):
                mm(pb_[:, 0:512], combcm[:, lc, tc * 128:(tc + 1) * 128],
                   f2T[:, lc, 512:1024], lc == 0, lc == 3, reads=[b_combcm, b_f2T], writes=[bb_])
            tt(rt3[:, 0:512], pa_[:, 0:512], gb_b[:, 0:512], OP.mult,
               reads=[ba_, b_gbb], writes=[b_rt3])
            tt(rt3[:, 512:1024], pb_[:, 0:512], gb_b[:, 512:1024], OP.mult,
               reads=[bb_, b_gbb], writes=[b_rt3])
            tt(outs[tc % 2][:, :], at[:, tc, :], rt3[:, :], OP.add,
               reads=[b_at, b_rt3], writes=[b_outs[tc % 2]])
            dmast(out_d[t0 + tc * 128:t0 + tc * 128 + 128, :], outs[tc % 2][:, :],
                  reads=[b_outs[tc % 2]])

    p.barrier()
    esC.close()

    with nc.Block() as block:
        p.replay(block)
    esg.close()
    return nc, p


def _prep_inputs(inputs, core):
    g = lambda k: np.ascontiguousarray(inputs[k])
    d = {
        "x": np.ascontiguousarray(inputs["L0_attn_hidden_states"][core]),
        "Wq": g("L0_attn_Wq"), "Wk": g("L0_attn_Wk"), "Wv": g("L0_attn_Wv"),
        "Wo": g("L0_attn_Wo"),
        "Wsm": np.ascontiguousarray(np.concatenate(
            [inputs["L0_attn_Wb"], inputs["L0_attn_Wdec"], inputs["L0_attn_Wmix"]], axis=1)),
        "convq": g("L0_attn_conv_q"), "convk": g("L0_attn_conv_k"),
        "convv": g("L0_attn_conv_v"), "convm": g("L0_attn_conv_mix"),
        "mixb": np.ascontiguousarray(inputs["L0_attn_mix_bias"].reshape(4, 1)),
        "rmsw": np.ascontiguousarray(inputs["L0_attn_rms_w"].reshape(2, 128)),
        "gate": g("L0_ffn_gate"),
        "corrb": np.ascontiguousarray(inputs["L0_ffn_corr_bias"].reshape(1, 8)),
        "w1": g("L0_ffn_w1"), "w2": g("L0_ffn_w2"),
        "fc1": g("L0_ffn_fc1_w"), "fc2": g("L0_ffn_fc2_w"),
        "up": g("L0_ffn_up_w"), "down": g("L0_ffn_down_w"),
        "cvec": np.ascontiguousarray(inputs["L0_c"][core].reshape(1, 1024)),
        "wmod": g("L0_wmod"),
    }
    return d


_CACHE = {}


def kernel(**inputs):
    key = "full"
    if key not in _CACHE:
        _CACHE[key] = build(n_tiles=16)
    nc, _ = _CACHE[key]
    in_maps = [_prep_inputs(inputs, c) for c in range(8)]
    res = run_bass_kernel_spmd(nc, in_maps, core_ids=list(range(8)))
    return np.stack([np.asarray(r["out"]) for r in res.results], axis=0)


if __name__ == "__main__":
    nc, p = build(n_tiles=2)
    print({e: p.cnt[e] for e in ENGS}, p.dcnt)
